# version 16 (final)
# baseline (speedup 1.0000x reference)
import numpy as np
import ml_dtypes
import concourse.bass as bass
import concourse.mybir as mybir
from concourse.bass_utils import run_bass_kernel_spmd

F32 = mybir.dt.float32
BF16 = mybir.dt.bfloat16
AF = mybir.ActivationFunctionType
ALU = mybir.AluOpType
AX = mybir.AxisListType
BF = ml_dtypes.bfloat16

NCORES = 8
D = 4096
S = 8192
TPC = 1024
NSLOT = 8
FC = 32
HD = 128
EPS = 1e-5
RS = float(1.0 / np.sqrt(HD))
NEG = -1.0e9


def tok_ids(c):
    return np.concatenate([np.arange((8 * j + c) * 128, (8 * j + c) * 128 + 128)
                           for j in range(NSLOT)])


def tile_w(W):
    K, O = W.shape
    kc, oc = K // 128, O // 128
    return np.ascontiguousarray(
        W.reshape(kc, 128, oc, 128).transpose(2, 1, 0, 3).reshape(oc, 128, K))


def _llama3_inv_freq():
    THETA, FACTOR, HI_FF, LO_FF, ORIG_MAX = 500000.0, 8.0, 4.0, 1.0, 8192
    inv = 1.0 / THETA ** (np.arange(0, HD, 2, dtype=np.float64) / HD)
    wavelen = 2.0 * np.pi / inv
    low_wl = ORIG_MAX / LO_FF
    high_wl = ORIG_MAX / HI_FF
    smooth = (ORIG_MAX / wavelen - LO_FF) / (HI_FF - LO_FF)
    scaled = np.where(wavelen > low_wl, inv / FACTOR, inv)
    mid = (wavelen <= low_wl) & (wavelen >= high_wl)
    scaled = np.where(mid, (1 - smooth) * inv / FACTOR + smooth * inv, scaled)
    return scaled.astype(np.float32)


def rope_tables(c, scale):
    inv = _llama3_inv_freq().astype(np.float64)
    pos = tok_ids(c).astype(np.float64)
    fr = pos[None, :] * inv[:, None]
    cos = np.cos(fr)
    sin = np.sin(fr)
    cost = (np.concatenate([cos, cos], axis=0) * scale).astype(BF)
    sint = (np.concatenate([-sin, sin], axis=0) * scale).astype(BF)
    return cost, sint


def _alibi_slopes():
    m = np.arange(1, 33, dtype=np.float64) * (8.0 / 32)
    return 1.0 / 2.0 ** m


def _bf16_split3(x):
    h1 = x.astype(BF)
    r = x - h1.astype(np.float64)
    h2 = r.astype(BF)
    h3 = (r - h2.astype(np.float64)).astype(BF)
    return h1, h2, h3


def alibi_lhs_packed():
    out = np.zeros((73, 22 * 128), dtype=BF)
    for b in range(64):
        kg = (b * 128 + np.arange(128)).astype(np.float64)
        kgh, kgl, _ = _bf16_split3(kg)
        pbase = 32 * (b % 3)
        cs = slice((b // 3) * 128, (b // 3) * 128 + 128)
        tile = np.zeros((9, 128), dtype=BF)
        tile[0] = tile[1] = tile[2] = kgh
        tile[3] = tile[4] = tile[5] = kgl
        tile[6] = tile[7] = tile[8] = BF(1.0)
        out[pbase:pbase + 9, cs] = tile
    return out


def alibi_rhs(c):
    sl = _alibi_slopes()
    out = np.zeros((32, 2, 9, 512), dtype=BF)
    for h in range(32):
        sh, sl_, sl2 = _bf16_split3(np.float64(sl[h]))
        for grp in range(2):
            for jj in range(4):
                j = grp * 4 + jj
                qg = ((8 * j + c) * 128 + np.arange(128)).astype(np.float64)
                th, tl, _ = _bf16_split3(sl[h] * qg)
                cs = slice(jj * 128, jj * 128 + 128)
                out[h, grp, 0, cs] = sh
                out[h, grp, 1, cs] = sl_
                out[h, grp, 2, cs] = sl2
                out[h, grp, 3, cs] = sh
                out[h, grp, 4, cs] = sl_
                out[h, grp, 5, cs] = sl2
                out[h, grp, 6, cs] = -th
                out[h, grp, 7, cs] = -tl
    return out


def causal_masks(c):
    out = np.zeros((8, 128, 128), dtype=np.float32)
    tri = np.where(np.arange(128)[:, None] > np.arange(128)[None, :], NEG, 0.0)
    for t in range(8):
        if t == c:
            out[t] = tri.astype(np.float32)
        elif t > c:
            out[t] = NEG
    return out



class Em:
    ENG = ("sync", "tensor", "vector", "scalar", "gpsimd")

    def __init__(self, nc):
        self.nc = nc
        self.q = {e: [] for e in self.ENG}
        self.cnt = {}
        self.handles = {}
        self.lastw = {e: {} for e in self.ENG}

    def sem(self, name):
        if name not in self.handles:
            self.handles[name] = self.nc.alloc_semaphore(name)
            self.cnt[name] = 0
        return name

    def fence(self, name):
        return (name, self.cnt[name]) if self.cnt[name] > 0 else None

    serialize = False

    def emit(self, eng, fn, waits=None, inc=None, amt=1):
        if Em.serialize:
            waits = list(waits or []) + [self.fence(n) for n in self.cnt]
        w = []
        if waits:
            for t in waits:
                if t is None:
                    continue
                s, v = t
                if v > self.lastw[eng].get(s, 0):
                    self.lastw[eng][s] = v
                    w.append((s, v))
        tok = None
        if inc:
            self.cnt[inc] += amt
            tok = (inc, self.cnt[inc])
        self.q[eng].append((w, fn, (inc, amt) if inc else None))
        return tok

    def barrier(self):
        toks = [(n, v) for n, v in self.cnt.items() if v > 0]
        for eng in self.ENG:
            if self.q[eng]:
                self.emit(eng, None, waits=toks)

    def dma(self, eng, out, in_, sem, waits=None):
        return self.emit(eng, lambda E, o=out, i=in_: E.dma_start(out=o, in_=i),
                         waits=waits, inc=sem, amt=16)

    def finalize(self):
        nc = self.nc
        with nc.Block() as block:
            for eng in self.ENG:
                entries = self.q[eng]
                if not entries:
                    continue

                def make(entries=entries):
                    def body(E):
                        for w, fn, inc in entries:
                            for s, v in w:
                                E.wait_ge(self.handles[s], v)
                            ins = fn(E) if fn is not None else None
                            if inc is not None:
                                ins.then_inc(self.handles[inc[0]], inc[1])
                    return body
                getattr(block, eng)(make())



class Ctx:
    def __init__(self, nc, kind):
        self.nc = nc
        self.kind = kind
        em = self.em = Em(nc)
        for s in ("PE", "DV", "AC", "QLD", "QW", "QR0", "QR1", "QV0", "QV1",
                  "QK", "QQ", "QO", "QST", "QSQ", "QSV", "KS0", "KS1", "QSW",
                  "QSO", "QSR0", "QSR1", "QV2", "QV3", "QS2", "QS3",
                  "QV4", "QV5", "QV6", "QV7", "QQ0", "QQ1") + (
                      ("CC0", "CC1", "CC2", "CC3") if kind == "F" else ()):
            em.sem(s)
        A_launch = kind.startswith("A")
        self.A = nc.alloc_sbuf_tensor("arena_a", [128, FC * 1024], BF16)
        self.B = nc.alloc_sbuf_tensor("arena_b", [128, FC * 1024], BF16)
        self.WF = nc.alloc_sbuf_tensor("wf", [128, 4096], F32)
        self.WB = [nc.alloc_sbuf_tensor(f"wb{i}", [128, 4096], BF16)
                   for i in range(2)]
        self.RF = [nc.alloc_sbuf_tensor(f"rf{i}", [128, 1024], F32)
                   for i in range(2)]
        self.STAT = nc.alloc_sbuf_tensor("stat", [1, 2048], F32)
        self.ABR = nc.alloc_sbuf_tensor("abr", [1, 2048], BF16)
        self.AREP = nc.alloc_sbuf_tensor("arep", [128, 2048], BF16)
        if kind == "F":
            SVT = nc.alloc_sbuf_tensor("svt", [128, 4096], BF16)
            self.STG = SVT
            self.M2 = SVT[:, 2048:3072]
            self.VTE = SVT[:, 3072:4096]
            RALB = nc.alloc_sbuf_tensor("ralb", [128, 4096], BF16)
            self.RTAB = RALB
            self.ALB = RALB[0:73, 0:2816]
            self.ALR = RALB[0:73, 2816:3840]
            self.ROUT = [nc.alloc_sbuf_tensor("ro0", [128, 1024], F32)]
            self.GLN = nc.alloc_sbuf_tensor("gln", [128, 128], F32)
            self.B12 = nc.alloc_sbuf_tensor("b12s", [128, 64], F32)
            self.B3 = nc.alloc_sbuf_tensor("b3s", [128, 32], F32)
        if A_launch:
            self.STG = nc.alloc_sbuf_tensor("stg", [128, 3072], BF16)
            VT = nc.alloc_sbuf_tensor("vt", [128, 2048], BF16)
            self.M2 = VT[:, 1024:2048]
            self.VTE = VT[:, 0:1024]
        elif kind != "F":
            self.STG = nc.alloc_sbuf_tensor("stg", [128, 2048], BF16)
            self.ROUT = [nc.alloc_sbuf_tensor(f"ro{i}", [128, 1024], F32)
                         for i in range(2)]
        if kind == "A1":
            self.GLN = nc.alloc_sbuf_tensor("gln", [128, 128], F32)
        if kind == "B1":
            self.ALB = nc.alloc_sbuf_tensor("alb", [73, 22 * 128], BF16)[:, :]
            self.ALR = nc.alloc_sbuf_tensor("alr_sb", [73, 1024], BF16)[:, :]
            self.B12 = nc.alloc_sbuf_tensor("b12s", [128, 64], F32)
            self.B3 = nc.alloc_sbuf_tensor("b3s", [128, 32], F32)
        if kind == "A0":
            self.RTAB = nc.alloc_sbuf_tensor("rtab", [128, 4096], BF16)
        self.FSC = nc.alloc_sbuf_tensor("fsc", [128, 32], F32)
        self.P = [nc.alloc_psum_tensor(f"p{i}", [128, 512], F32) for i in range(7)]
        self.PT = nc.alloc_psum_tensor("ptb", [128, 1024], BF16)
        oc_ = nc.inline_tensor(np.ones((128, 1), dtype=BF), "c_ones_col")
        or_ = nc.inline_tensor(np.ones((1, 128), dtype=BF), "c_ones_row")
        ib_ = nc.inline_tensor(np.eye(128, dtype=np.float32).astype(BF),
                               "c_ident_b")
        if_ = nc.inline_tensor(np.eye(128, dtype=np.float32), "c_ident_f")
        self.c_ones_col = nc.alloc_sbuf_tensor("s_ones_col", [128, 1], BF16)
        self.c_ones_row = nc.alloc_sbuf_tensor("s_ones_row", [1, 128], BF16)
        self.c_ident_b = nc.alloc_sbuf_tensor("s_ident_b", [128, 128], BF16)
        self.c_ident_f = nc.alloc_sbuf_tensor("s_ident_f", [128, 128], F32)
        em.dma("sync", self.c_ones_col[:], oc_.ap(), "QLD")
        em.dma("sync", self.c_ones_row[:], or_.ap(), "QLD")
        em.dma("sync", self.c_ident_b[:], ib_.ap(), "QLD")
        self.tok_consts = em.dma("sync", self.c_ident_f[:], if_.ap(), "QLD")
        self.vt_free = None
        self.rope_m2_free = None



def emit_ln(ctx, r_src, dst_arena, gamma=None, beta=None, src_arena=None,
            out_cb=None, src_toks=None, first_waits=None):
    em = ctx.em
    P = ctx.P
    xb = src_arena if src_arena is not None else ctx.B
    stream = src_arena is None
    SQ = ctx.STG
    cast_toks = [None] * FC
    sq_toks = [None] * FC
    statmm_toks = [None] * FC

    for fc in range(FC):
        sqc = SQ[:, (fc % 2) * 1024:(fc % 2 + 1) * 1024]
        if stream:
            rbuf = ctx.RF[fc % 2]
            w = ([cast_toks[fc - 2], sq_toks[fc - 2]] if fc >= 2
                 else list(first_waits or []))
            tok_ld = em.dma("sync", rbuf[:], r_src[fc * 128:(fc + 1) * 128, :],
                            "QR%d" % (fc % 2), waits=w)
            cast_toks[fc] = em.emit(
                "scalar", lambda E, s=rbuf, d=xb[:, fc * 1024:(fc + 1) * 1024]:
                E.activation(out=d, in_=s[:], func=AF.Copy),
                waits=[tok_ld], inc="AC")
            sq_toks[fc] = em.emit(
                "scalar", lambda E, s=rbuf, d=sqc:
                E.activation(out=d, in_=s[:], func=AF.Square),
                waits=([statmm_toks[fc - 2]] if fc >= 2 else list(first_waits
                                                                 or [])),
                inc="AC")
            src_tok = cast_toks[fc]
        else:
            src_tok = src_toks[fc] if src_toks else None
            xc_ = xb[:, fc * 1024:(fc + 1) * 1024]
            sq_toks[fc] = em.emit(
                "vector", lambda E, x=xc_, d=sqc: E.tensor_tensor(d, x, x,
                                                                  ALU.mult),
                waits=[src_tok] + ([statmm_toks[fc - 2]] if fc >= 2
                                   else list(first_waits or [])),
                inc="DV")
        xc = xb[:, fc * 1024:(fc + 1) * 1024]
        for h in range(2):
            em.emit("tensor",
                    lambda E, r=xc[:, h * 512:(h + 1) * 512], o=P[2 + h][0:1, :],
                    st=(fc == 0), sp=(fc == FC - 1):
                    E.matmul(o, ctx.c_ones_col[:], r, start=st, stop=sp),
                    waits=[src_tok, ctx.tok_consts])
        for h in range(2):
            statmm_toks[fc] = em.emit(
                "tensor",
                lambda E, r=sqc[:, h * 512:(h + 1) * 512], o=P[4 + h][0:1, :],
                st=(fc == 0), sp=(fc == FC - 1):
                E.matmul(o, ctx.c_ones_col[:], r, start=st, stop=sp),
                waits=[sq_toks[fc]], inc="PE" if h == 1 else None)
    stat_done = statmm_toks[FC - 1]

    ST = ctx.STAT
    mu, u, t_ = ST[0:1, 0:512], ST[0:1, 512:1024], ST[0:1, 1024:1536]
    tprev = None
    for h in range(2):
        tk = em.emit("vector", lambda E, h=h: E.tensor_scalar(
            mu[:], P[2 + h][0:1, :], 1.0 / D, None, ALU.mult),
            waits=[stat_done, tprev], inc="DV")
        tk = em.emit("vector", lambda E: E.tensor_tensor(u[:], mu[:], mu[:],
                                                         ALU.mult), inc="DV")
        tk = em.emit("vector", lambda E, h=h: E.tensor_scalar(
            t_[:], P[4 + h][0:1, :], 1.0 / D, None, ALU.mult), inc="DV")
        tk = em.emit("vector", lambda E: E.tensor_tensor(t_[:], t_[:], u[:],
                                                         ALU.subtract), inc="DV")
        tk = em.emit("vector", lambda E: E.tensor_scalar(
            t_[:], t_[:], EPS, None, ALU.add), inc="DV")
        tk = em.emit("scalar", lambda E: E.activation(out=u[:], in_=t_[:],
                                                      func=AF.Ln),
                     waits=[tk], inc="AC")
        tk = em.emit("vector", lambda E: E.tensor_scalar(
            u[:], u[:], -0.5, None, ALU.mult), waits=[tk], inc="DV")
        tk = em.emit("scalar", lambda E: E.activation(out=t_[:], in_=u[:],
                                                      func=AF.Exp),
                     waits=[tk], inc="AC")
        em.emit("scalar", lambda E, h=h: E.activation(
            out=ctx.ABR[:, h * 512:(h + 1) * 512], in_=t_[:], func=AF.Copy),
            inc="AC")
        tk = em.emit("vector", lambda E: E.scalar_tensor_tensor(
            u[:], mu[:], -1.0, t_[:], ALU.mult, ALU.mult), waits=[tk], inc="DV")
        tprev = em.emit("scalar", lambda E, h=h: E.activation(
            out=ctx.ABR[:, 1024 + h * 512:1024 + (h + 1) * 512], in_=u[:],
            func=AF.Copy), waits=[tk], inc="AC")

    ev_toks = []
    for i in range(4):
        tmm = em.emit("tensor", lambda E, i=i:
                      E.matmul(ctx.P[2 + i][:], ctx.c_ones_row[:],
                               ctx.ABR[:, i * 512:(i + 1) * 512],
                               start=True, stop=True),
                      waits=[tprev], inc="PE")
        dst = ctx.AREP[:, i * 512:(i + 1) * 512]
        ev_toks.append(em.emit("scalar", lambda E, s=ctx.P[2 + i], d=dst:
                               E.activation(out=d, in_=s[:], func=AF.Copy),
                               waits=[tmm], inc="AC"))
    arep = ctx.AREP[:, 0:1024]
    brep = ctx.AREP[:, 1024:2048]

    out_toks = []
    stg_busy = [None, None]
    for fc in range(FC):
        xc = xb[:, fc * 1024:(fc + 1) * 1024]
        if dst_arena is not None:
            oc_ = dst_arena[:, fc * 1024:(fc + 1) * 1024]
            wslot = None
        else:
            oc_ = ctx.STG[:, (fc % 2) * 1024:(fc % 2 + 1) * 1024]
            wslot = stg_busy[fc % 2]
        t2 = em.emit("vector", lambda E, x=xc, d=oc_:
                     E.tensor_tensor(d, x, arep, ALU.mult),
                     waits=(ev_toks if fc == 0 else []) +
                     ([wslot] if wslot else []),
                     inc="DV")
        t2 = em.emit("vector", lambda E, d=oc_:
                     E.tensor_tensor(d, d, brep, ALU.add), inc="DV")
        if gamma is not None:
            t2 = em.emit("vector", lambda E, d=oc_, f=fc:
                         E.tensor_scalar(d, d, gamma[:, f:f + 1],
                                         beta[:, f:f + 1], ALU.mult, ALU.add),
                         waits=[t2], inc="DV")
        if out_cb is not None:
            t2 = out_cb(fc, oc_, t2)
            stg_busy[fc % 2] = t2
        out_toks.append(t2)
    return out_toks


class WStream:

    def __init__(self, ctx, seq, init_waits=None):
        self.ctx = ctx
        self.seq = seq
        self.init_waits = list(init_waits or [])
        self.dma_toks = [None] * len(seq)
        self.cast_toks = [None] * len(seq)
        self.free_toks = [None] * len(seq)
        self.issued = 0

    def _issue(self, i):
        ctx, em = self.ctx, self.ctx.em
        wt, oc = self.seq[i]
        w = [self.cast_toks[i - 1]] if i >= 1 else list(self.init_waits)
        self.dma_toks[i] = em.dma("sync", ctx.WF[:], wt[oc], "QW", waits=w)
        cw = [self.dma_toks[i]]
        if i >= 2:
            cw.append(self.free_toks[i - 2])
        self.cast_toks[i] = em.emit(
            "scalar", lambda E, s=ctx.WF, d=ctx.WB[i % 2]:
            E.activation(out=d[:], in_=s[:], func=AF.Copy), waits=cw, inc="AC")
        self.issued = i + 1

    def get(self, i):
        while self.issued <= min(i + 1, len(self.seq) - 1):
            self._issue(self.issued)
        return self.ctx.WB[i % 2], self.cast_toks[i]

    def done(self, i, pe_tok):
        self.free_toks[i] = pe_tok


def emit_mm_oc(ctx, wb, wtok, rhs_arena, psums, extra_waits=None):
    em = ctx.em
    tok = None
    for h in range(2):
        for kc in range(FC):
            w = ([wtok] + (extra_waits or [])) if kc == 0 else None
            tok = em.emit(
                "tensor",
                lambda E, l=wb[:, kc * 128:(kc + 1) * 128],
                r=rhs_arena[:, kc * 1024 + h * 512: kc * 1024 + h * 512 + 512],
                o=psums[h], st=(kc == 0), sp=(kc == FC - 1):
                E.matmul(o[:], l, r, start=st, stop=sp),
                waits=w, inc="PE" if kc == FC - 1 else None)
    return tok


def emit_rope_full(ctx, src, cost, sint, dst, waits):
    em = ctx.em
    m2 = ctx.M2
    ta = em.dma("gpsimd", m2[0:64, :], src[64:128, :], "QSW",
                waits=waits + [ctx.rope_m2_free])
    tb = em.dma("gpsimd", m2[64:128, :], src[0:64, :], "QSW")
    em.emit("vector", lambda E: E.tensor_tensor(dst, src, cost, ALU.mult),
            waits=waits, inc="DV")
    em.emit("vector", lambda E: E.tensor_tensor(m2, m2, sint, ALU.mult),
            waits=[ta, tb], inc="DV")
    tok = em.emit("vector", lambda E: E.tensor_tensor(dst, dst, m2, ALU.add),
                  inc="DV")
    ctx.rope_m2_free = tok
    return tok


def emit_vtrans(ctx, src_view, v_out, kv, waits):
    em = ctx.em
    ttok = None
    for t in range(8):
        ttok = em.emit("tensor", lambda E, t=t:
                       E.transpose(ctx.PT[:, t * 128:(t + 1) * 128],
                                   src_view[:, t * 128:(t + 1) * 128],
                                   ctx.c_ident_b[:]),
                       waits=(waits + [ctx.vt_free]) if t == 0 else None,
                       inc="PE" if t == 7 else None)
    tev = em.emit("scalar", lambda E: E.activation(out=ctx.VTE,
                                                   in_=ctx.PT[:], func=AF.Copy),
                  waits=[ttok], inc="AC")
    td = None
    for t in range(8):
        td = em.dma("gpsimd",
                    v_out[t * 128:(t + 1) * 128, kv * 128:(kv + 1) * 128],
                    ctx.VTE[:, t * 128:(t + 1) * 128], "QSV",
                    waits=[tev] if t == 0 else None)
    ctx.vt_free = td
    return td


def emit_out_proj(ctx, wt, rhs_arena, r_src, r_dst, bias_sb, rhs_ready,
                  init_waits):
    em = ctx.em
    ws = WStream(ctx, [(wt, i) for i in range(32)], init_waits=init_waits)
    evac_dma = [None, None]
    evac_dv = [None, None]
    for i in range(32):
        wb, wtok = ws.get(i)
        p = [ctx.P[(i % 2) * 2], ctx.P[(i % 2) * 2 + 1]]
        ew = [rhs_ready] if i == 0 else [evac_dv[i % 2]]
        ptok = emit_mm_oc(ctx, wb, wtok, rhs_arena, p, extra_waits=ew)
        ws.done(i, ptok)
        rbuf = ctx.RF[i % 2]
        rw = [evac_dv[i % 2]] if i >= 2 else list(init_waits)
        tok_r = em.dma("sync", rbuf[:], r_src[i * 128:(i + 1) * 128, :],
                       "QR%d" % (i % 2), waits=rw)
        nR = len(ctx.ROUT)
        obuf = ctx.ROUT[i % nR]
        tts = []
        for h in range(2):
            if bias_sb is not None:
                tts.append(em.emit(
                    "vector", lambda E, h=h, i=i, o=obuf, r=rbuf, p=p:
                    E.scalar_tensor_tensor(o[:, h * 512:(h + 1) * 512], p[h][:],
                                           bias_sb[:, i:i + 1],
                                           r[:, h * 512:(h + 1) * 512],
                                           ALU.add, ALU.add),
                    waits=[ptok, tok_r, evac_dma[i % nR]], inc="DV"))
            else:
                tts.append(em.emit(
                    "vector", lambda E, h=h, i=i, o=obuf, r=rbuf, p=p:
                    E.tensor_tensor(o[:, h * 512:(h + 1) * 512], p[h][:],
                                    r[:, h * 512:(h + 1) * 512], ALU.add),
                    waits=[ptok, tok_r, evac_dma[i % nR]], inc="DV"))
        td = em.dma("gpsimd", r_dst[i * 128:(i + 1) * 128, :], obuf[:],
                    "QSR%d" % (i % nR), waits=tts)
        evac_dma[i % nR] = td
        evac_dv[i % 2] = tts[-1]



def _clip2(E, dst, p0, p1):
    E.tensor_scalar(dst[:, 0:512], p0[:], 8.0, -8.0, ALU.min, ALU.max)
    return E.tensor_scalar(dst[:, 512:1024], p1[:], 8.0, -8.0, ALU.min, ALU.max)


def build_launchA(layer):
    nc = bass.Bass("TRN2", target_bir_lowering=False)
    aps = {}
    aps["r_in"] = nc.dram_tensor("r_in", [D, TPC], F32, kind="ExternalInput").ap()
    if layer == 0:
        aps["wq_t"] = nc.dram_tensor("wq_t", [32, 128, D], F32, kind="ExternalInput").ap()
        aps["wk_t"] = nc.dram_tensor("wk_t", [8, 128, D], F32, kind="ExternalInput").ap()
        aps["wv_t"] = nc.dram_tensor("wv_t", [8, 128, D], F32, kind="ExternalInput").ap()
        aps["ropes"] = nc.dram_tensor("ropes", [4, 128, TPC], BF16, kind="ExternalInput").ap()
        aps["q_out"] = nc.dram_tensor("q_out", [D, TPC], BF16, kind="ExternalOutput").ap()
        aps["k_out"] = nc.dram_tensor("k_out", [1024, TPC], BF16, kind="ExternalOutput").ap()
        aps["v_out"] = nc.dram_tensor("v_out", [TPC, 1024], BF16, kind="ExternalOutput").ap()
    else:
        aps["wqkv_t"] = nc.dram_tensor("wqkv_t", [96, 128, D], F32, kind="ExternalInput").ap()
        aps["lnp"] = nc.dram_tensor("lnp", [128, 128], F32, kind="ExternalInput").ap()
        aps["q_out"] = nc.dram_tensor("q_out", [D, TPC], BF16, kind="ExternalOutput").ap()
        aps["k_out"] = nc.dram_tensor("k_out", [D, TPC], BF16, kind="ExternalOutput").ap()
        aps["v_out"] = nc.dram_tensor("v_out", [TPC, D], BF16, kind="ExternalOutput").ap()
    ctx = Ctx(nc, "A%d" % layer)
    emit_phaseA(ctx, layer, aps)
    _final_barrier(ctx.em)
    ctx.em.finalize()
    return nc


def emit_phaseA(ctx, layer, aps):
    em = ctx.em
    nc = ctx.nc
    r_in = aps["r_in"]
    q_out, k_out, v_out = aps["q_out"], aps["k_out"], aps["v_out"]

    if layer == 0:
        for a in range(4):
            tok_rt = em.dma("sync", ctx.RTAB[:, a * 1024:(a + 1) * 1024],
                            aps["ropes"][a], "QLD")
    else:
        tok_ln = em.dma("sync", ctx.GLN[:], aps["lnp"][:, :], "QLD")
        tok_ln = em.emit("vector", lambda E: E.tensor_scalar(
            ctx.GLN[:, 64:128], ctx.GLN[:, 64:128], RS, None, ALU.mult),
            waits=[tok_ln], inc="DV")

    ln_toks = emit_ln(ctx, r_in, ctx.A)
    ln_done = ln_toks[-1]
    fac0 = em.fence("AC")

    if layer == 0:
        ws = WStream(ctx, [(aps["wq_t"], i) for i in range(32)]
                     + [(aps["wk_t"], i) for i in range(8)]
                     + [(aps["wv_t"], i) for i in range(8)])
        prev_done = None
        for i in range(48):
            wb, wtok = ws.get(i)
            ew = [ln_done, fac0] if i == 0 else [prev_done]
            ptok = emit_mm_oc(ctx, wb, wtok, ctx.A, [ctx.P[0], ctx.P[1]],
                              extra_waits=ew)
            ws.done(i, ptok)
            if i < 40:
                tv = None
                for h in range(2):
                    tv = em.emit(
                        "scalar", lambda E, h=h:
                        E.activation(out=ctx.STG[:, h * 512:(h + 1) * 512],
                                     in_=ctx.P[h][:], func=AF.Copy),
                        waits=[ptok], inc="AC")
                if i < 32:
                    cost, sint = ctx.RTAB[:, 0:1024], ctx.RTAB[:, 1024:2048]
                    dst_dram = q_out[i * 128:(i + 1) * 128, :]
                else:
                    cost, sint = ctx.RTAB[:, 2048:3072], ctx.RTAB[:, 3072:4096]
                    dst_dram = k_out[(i - 32) * 128:(i - 31) * 128, :]
                tr = emit_rope_full(ctx, ctx.STG[:, 0:1024], cost, sint,
                                    ctx.STG[:, 1024:2048], [tv, tok_rt])
                prev_done = em.dma("gpsimd", dst_dram, ctx.STG[:, 1024:2048],
                                   "QSQ", waits=[tr])
            else:
                kv = i - 40
                tv = None
                for h in range(2):
                    tv = em.emit(
                        "scalar", lambda E, h=h:
                        E.activation(
                            out=ctx.STG[:, 2048 + h * 512:2048 + (h + 1) * 512],
                            in_=ctx.P[h][:], func=AF.Copy),
                        waits=[ptok], inc="AC")
                prev_done = emit_vtrans(ctx, ctx.STG[:, 2048:3072], v_out,
                                        kv, [tv])
    else:
        ws = WStream(ctx, [(aps["wqkv_t"], oc) for oc in
                           list(range(32, 96)) + list(range(0, 32))])
        prev_done = None
        k_toks = [None] * FC
        q_toks = [None] * FC
        b_free = [None] * FC
        for i in range(96):
            wb, wtok = ws.get(i)
            ew = [ln_done, fac0] if i == 0 else [prev_done]
            ptok = emit_mm_oc(ctx, wb, wtok, ctx.A, [ctx.P[0], ctx.P[1]],
                              extra_waits=ew)
            ws.done(i, ptok)
            oc = ws.seq[i][1]
            if 32 <= oc < 64:
                fc = oc - 32
                tcl = em.emit("vector",
                              lambda E, d=ctx.B[:, fc * 1024:(fc + 1) * 1024]:
                              _clip2(E, d, ctx.P[0], ctx.P[1]),
                              waits=[ptok], inc="DV", amt=2)
                k_toks[fc] = tcl
                prev_done = tcl
            elif oc >= 64:
                kv = oc - 64
                tcl = em.emit("vector", lambda E, d=ctx.STG[:, 2048:3072]:
                              _clip2(E, d, ctx.P[0], ctx.P[1]),
                              waits=[ptok, ctx.vt_free], inc="DV", amt=2)
                prev_done = emit_vtrans(ctx, ctx.STG[:, 2048:3072], v_out,
                                        kv, [tcl])
                if kv == 0:
                    def k_cb(fc2, view, tok2):
                        t = em.dma("gpsimd", k_out[fc2 * 128:(fc2 + 1) * 128, :],
                                   view, "KS%d" % (fc2 % 2), waits=[tok2])
                        b_free[fc2] = t
                        return t
                    emit_ln(ctx, None, None, gamma=ctx.GLN[:, 64:96],
                            beta=ctx.GLN[:, 96:128], src_arena=ctx.B,
                            out_cb=k_cb, src_toks=k_toks, first_waits=[tok_ln])
            else:
                fc = oc
                tcl = em.emit("vector",
                              lambda E, d=ctx.B[:, fc * 1024:(fc + 1) * 1024]:
                              _clip2(E, d, ctx.P[0], ctx.P[1]),
                              waits=[ptok, b_free[fc]], inc="DV", amt=2)
                q_toks[fc] = tcl
                prev_done = tcl

        def q_cb(fc2, view, tok2):
            return em.dma("gpsimd", q_out[fc2 * 128:(fc2 + 1) * 128, :],
                          view, "QS%d" % (2 + fc2 % 2), waits=[tok2])
        emit_ln(ctx, None, None, gamma=ctx.GLN[:, 0:32], beta=ctx.GLN[:, 32:64],
                src_arena=ctx.B, out_cb=q_cb, src_toks=q_toks,
                first_waits=[prev_done])

    return


def _final_barrier(em):
    sems = ["QST", "QSQ", "QSV", "KS0", "KS1", "QSW", "QSO", "QSR0", "QSR1",
            "QS2", "QS3"]
    em.emit("gpsimd", None, waits=[(n, em.cnt[n]) for n in sems
                                   if em.cnt[n] > 0])



def emit_attention(ctx, layer, q_in, kg, vg, o_dram, tok_masks, tok_alibi,
                   only=None, rank_major=False, kv_ready=None):
    em = ctx.em
    P = ctx.P
    KF = 1024 if layer == 0 else 4096
    if layer == 0:
        ngroups, heads_per = 4, 8
        kvslot_of = lambda qi: qi // 4
    else:
        ngroups, heads_per = 16, 2
        kvslot_of = lambda qi: qi

    WF = ctx.WF
    DEPTH = 4
    sub = [WF[:, i * 512:(i + 1) * 512] for i in range(4)]
    mrep = WF[:, 3072:3584]
    linvrep = WF[:, 3584:4096]
    MASKS = WF[:, 2048:3072]
    qcur2 = [ctx.B[:, 0:1024], ctx.B[:, 1024:2048]]
    NV = 8
    vtile = [ctx.B[:, 2048 + i * 256: 2048 + (i + 1) * 256] for i in range(NV)]
    pt = [ctx.B[:, 6144 + i * 512: 6144 + (i + 1) * 512] for i in range(4)]
    oevac = ctx.B[:, 8192:8704]
    SP = [ctx.P[0], ctx.P[1], ctx.P[3], ctx.P[4]]
    P1R = [ctx.P[0], ctx.P[1], ctx.P[3], ctx.P[4], ctx.P[2], ctx.P[5], ctx.P[6]]
    mnegc = ctx.FSC[:, 8:16]
    PCOL = ctx.FSC[:, 16:32]
    mrow_b = ctx.ABR[0:1, 0:1024]
    linv_f = ctx.STAT[0:1, 1536:2048]
    linv_b = ctx.ABR[0:1, 1024:1536]
    if layer == 1:
        ALRs = [ctx.ALR, ctx.B[0:73, 8704:9728]]

    def load_K(g, waits):
        base = (g % 2) * 16384
        tk = None
        for s_ in range(2):
            kvh = 2 * g + s_
            dst0 = base + s_ * 8192
            if not rank_major:
                tk = em.dma("sync", ctx.A[:, dst0:dst0 + 8192],
                            kg[kvh * 128:(kvh + 1) * 128, :], "QK",
                            waits=waits if s_ == 0 else None)
            else:
                for b_ in range(64):
                    tk = em.dma(
                        "sync",
                        ctx.A[:, dst0 + b_ * 128:dst0 + (b_ + 1) * 128],
                        kg[(b_ % 8) * KF + kvh * 128:
                           (b_ % 8) * KF + (kvh + 1) * 128,
                           (b_ // 8) * 128:(b_ // 8 + 1) * 128],
                        "QK", waits=waits if (s_ == 0 and b_ == 0) else None)
        return tk

    vfree = {i: None for i in range(NV)}
    qfree = {0: None, 1: None}
    alfree = {0: None, 1: None}
    grp_done = {}
    ktok = {0: load_K(0, [kv_ready[0]] if kv_ready else None)}
    kpend = {}

    def stage_K(g, waits):
        base = (g % 2) * 16384
        pairs = []
        for s_ in range(2):
            kvh = 2 * g + s_
            dst0 = base + s_ * 8192
            for b_ in range(64):
                pairs.append((dst0 + b_ * 128,
                              (b_ % 8) * KF + kvh * 128, (b_ // 8) * 128))
        kpend[g] = [pairs, waits]

    def issue_k(n=1):
        for g_ in sorted(kpend):
            pairs, w = kpend[g_]
            for _ in range(n):
                if not pairs:
                    break
                d0, r0, c0 = pairs.pop(0)
                ktok[g_] = em.dma("sync", ctx.A[:, d0:d0 + 128],
                                  kg[r0:r0 + 128, c0:c0 + 128], "QK",
                                  waits=w)
                kpend[g_][1] = None
                w = None
            if not pairs:
                del kpend[g_]
            return
    t4_prev = None
    prev_exp_last = None
    final_tok = None
    nheads = min(only, ngroups * heads_per) if only else ngroups * heads_per

    for g in range(ngroups):
        while g in kpend:
            issue_k(8)
        for qi in range(heads_per):
            if (qi == 1 and rank_major and g + 1 < ngroups
                    and (g + 1) * heads_per < nheads):
                stage_K(g + 1, [grp_done.get(g - 1)])
            h = g * heads_per + qi
            if h >= nheads:
                break
            kvs = kvslot_of(qi)
            base = (g % 2) * 16384
            kc = ctx.A[:, base + kvs * 8192: base + (kvs + 1) * 8192]
            par = h % 2
            qcur = qcur2[par]
            tok_q = em.dma("sync", qcur, q_in[h * 128:(h + 1) * 128, :],
                           "QQ%d" % par, waits=[qfree[par]])
            tok_k = ktok[g]
            if layer == 1:
                AL = ALRs[par]
                tok_ar = None
                for pb_ in (0, 32, 64):
                    for g_ in (0, 1):
                        tok_ar = em.dma("sync",
                                        AL[pb_:pb_ + 9,
                                           g_ * 512:(g_ + 1) * 512],
                                        ctx.alr_d[h, g_], "QLD",
                                        waits=[alfree[par]])
            else:
                AL = None
                tok_ar = None

            red = {}
            i_mm = 0
            for j in range(NSLOT):
                for kap in range(2 * (j + 1)):
                    b2 = i_mm % 7
                    w = [tok_q, tok_k]
                    if i_mm >= 7:
                        w.append(red[i_mm - 7])
                    else:
                        w += [t4_prev, prev_exp_last]
                    tmm = em.emit(
                        "tensor", lambda E, j=j, kap=kap, b2=b2, kc=kc,
                        qcur=qcur:
                        E.matmul(P1R[b2][:], qcur[:, j * 128:(j + 1) * 128],
                                 kc[:, kap * 512:(kap + 1) * 512],
                                 start=True, stop=True),
                        waits=w, inc="PE")
                    red[i_mm] = em.emit(
                        "vector", lambda E, b2=b2, kap=kap:
                        E.tensor_reduce(PCOL[:, kap:kap + 1], P1R[b2][:],
                                        AX.X, ALU.max),
                        waits=[tmm], inc="DV")
                    i_mm += 1
                em.emit(
                    "vector", lambda E, j=j, n=2 * (j + 1):
                    E.tensor_reduce(mnegc[:, j:j + 1], PCOL[:, 0:n],
                                    AX.X, ALU.max, negate=True),
                    inc="DV")
            tneg = em.fence("DV")

            rows_t = [None] * NSLOT
            TPP = [P[6], P[0]]
            for j in range(NSLOT):
                tt = em.emit("tensor", lambda E, j=j:
                             E.transpose(TPP[j % 2][0:1, 0:128],
                                         mnegc[:, j:j + 1], ctx.c_ident_f[:]),
                             waits=[tneg] + ([rows_t[j - 2]] if j >= 2 else []),
                             inc="PE")
                rows_t[j] = em.emit(
                    "vector", lambda E, j=j:
                    E.tensor_copy(mrow_b[:, j * 128:(j + 1) * 128],
                                  TPP[j % 2][0:1, 0:128]),
                    waits=[tt], inc="DV")
            trow = rows_t[-1]

            if layer == 1:
                tok_mr = None
                for pb_ in (0, 32, 64):
                    tok_mr = em.dma("gpsimd", AL[pb_ + 8:pb_ + 9, 0:1024],
                                    mrow_b[:, 0:1024], "QSW", waits=[trow])
            else:
                tok_mr = None

            for sg in range(2):
                jbase = 4 * sg
                maxkap = 2 * (jbase + 4)
                if layer == 0:
                    tb = em.emit("tensor", lambda E, sg=sg:
                                 E.matmul(P[6][:], ctx.c_ones_row[:],
                                          mrow_b[:, sg * 512:(sg + 1) * 512],
                                          start=True, stop=True),
                                 waits=[trow, t4_prev], inc="PE")
                    tmrep = em.emit("vector",
                                    lambda E: E.tensor_copy(mrep[:], P[6][:]),
                                    waits=[tb], inc="DV")
                else:
                    tmrep = None
                insts = []
                for kap in range(maxkap):
                    ja = max(jbase, (kap + 2) // 2 - 1)
                    col0 = (ja - jbase) * 128
                    N = 512 - col0
                    for bb in range(4):
                        insts.append((kap, bb, ja, col0, N))
                n_inst = len(insts)
                sub_t = {}
                exp_t = {}
                pvl_t = {}
                pend = {}
                vtok = {}
                LAG = 3
                LOOK = 4

                def vdma(i_):
                    kap, bb, ja, col0, N = insts[i_]
                    b = 4 * kap + bb
                    sl_ = i_ % NV
                    if not rank_major:
                        vsrc = vg[b * 128:(b + 1) * 128,
                                  g * 256:(g + 1) * 256]
                    else:
                        r0_ = (b % 8) * TPC + (b // 8) * 128
                        vsrc = vg[r0_:r0_ + 128, g * 256:(g + 1) * 256]
                    vtok[i_] = em.dma("sync", vtile[sl_][:], vsrc,
                                      "QV%d" % sl_,
                                      waits=[vfree[sl_]] +
                                      ([kv_ready[1]] if kv_ready else []))

                for i_ in range(min(LOOK, n_inst)):
                    vdma(i_)
                ib = 0
                for kap, bb, ja, col0, N in insts:
                    b = 4 * kap + bb
                    pb = ib % DEPTH
                    w = [tok_q, tok_k, tok_ar, tok_alibi]
                    if ib >= DEPTH:
                        w.append(exp_t[ib - DEPTH] if layer == 1
                                 else sub_t[ib - DEPTH])
                    else:
                        w += [trow, t4_prev, tok_mr, tmrep]
                    tmm = em.emit(
                        "tensor", lambda E, b=b, pb=pb, col0=col0, N=N,
                        jbase=jbase, kc=kc, sp_=(layer == 0), qcur=qcur:
                        E.matmul(SP[pb][:, col0:col0 + N],
                                 kc[:, b * 128:(b + 1) * 128],
                                 qcur[:, jbase * 128 + col0:
                                      jbase * 128 + col0 + N],
                                 start=True, stop=sp_),
                        waits=w, inc="PE")
                    if layer == 1:
                        pbase = 32 * (b % 3)
                        cs = (b // 3) * 128
                        tmm = em.emit(
                            "tensor", lambda E, pb=pb, col0=col0, N=N,
                            sg=sg, pbase=pbase, cs=cs, AL=AL:
                            E.matmul(SP[pb][:, col0:col0 + N],
                                     ctx.ALB[pbase:pbase + 9, cs:cs + 128],
                                     AL[pbase:pbase + 9,
                                        sg * 512 + col0:sg * 512 + col0 + N],
                                     start=False, stop=True),
                            inc="PE")
                        tsb = tmm
                        jd = b // 8
                        if ja <= jd < jbase + 4:
                            t_ = b % 8
                            cj = (jd - jbase) * 128
                            tsb = em.emit(
                                "vector", lambda E, pb=pb, cj=cj, t_=t_:
                                E.tensor_tensor(
                                    SP[pb][:, cj:cj + 128],
                                    SP[pb][:, cj:cj + 128],
                                    MASKS[:, t_ * 128:(t_ + 1) * 128],
                                    ALU.add),
                                waits=[tok_masks, tmm], inc="DV")
                        sub_t[ib] = tsb
                        exp_t[ib] = em.emit(
                            "scalar", lambda E, pb=pb, col0=col0, N=N:
                            E.activation(out=pt[pb][:, col0:col0 + N],
                                         in_=SP[pb][:, col0:col0 + N],
                                         func=AF.Exp),
                            waits=[tsb] + ([pvl_t[ib - DEPTH]]
                                           if ib >= DEPTH else []),
                            inc="AC")
                    else:
                        tsb = em.emit(
                            "vector", lambda E, pb=pb, col0=col0, N=N:
                            E.tensor_tensor(sub[pb][:, col0:col0 + N],
                                            SP[pb][:, col0:col0 + N],
                                            mrep[:, col0:col0 + N], ALU.add),
                            waits=[tmm, tmrep] +
                            ([exp_t[ib - DEPTH]] if ib >= DEPTH else []),
                            inc="DV")
                        jd = b // 8
                        if ja <= jd < jbase + 4:
                            t_ = b % 8
                            cj = (jd - jbase) * 128
                            tsb = em.emit(
                                "vector", lambda E, pb=pb, cj=cj, t_=t_:
                                E.tensor_tensor(
                                    sub[pb][:, cj:cj + 128],
                                    sub[pb][:, cj:cj + 128],
                                    MASKS[:, t_ * 128:(t_ + 1) * 128],
                                    ALU.add),
                                waits=[tok_masks], inc="DV")
                        sub_t[ib] = tsb
                        exp_t[ib] = em.emit(
                            "scalar", lambda E, pb=pb, col0=col0, N=N:
                            E.activation(out=pt[pb][:, col0:col0 + N],
                                         in_=sub[pb][:, col0:col0 + N],
                                         func=AF.Exp),
                            waits=[tsb] + ([pvl_t[ib - DEPTH]]
                                           if ib >= DEPTH else []),
                            inc="AC")
                    pend[ib] = (ib % NV, pb, col0, N, kvs,
                                (kap == 0 and bb == 0),
                                (kap == maxkap - 1 and bb == 3), vtok[ib])
                    if ib + LOOK < n_inst:
                        vdma(ib + LOOK)
                    if rank_major:
                        issue_k(1)
                    if ib - LAG >= 0:
                        _emit_pvl(ctx, pend, exp_t, pvl_t, vfree, pt,
                                  vtile, ib - LAG)
                    ib += 1
                for jb in range(max(0, ib - LAG), ib):
                    _emit_pvl(ctx, pend, exp_t, pvl_t, vfree, pt, vtile, jb)
                t1 = em.emit("vector", lambda E: E.reciprocal(linv_f[:],
                                                              P[5][0:1, :]),
                             waits=[pvl_t[ib - 1]], inc="DV")
                t1 = em.emit("vector", lambda E: E.tensor_copy(linv_b[:],
                                                               linv_f[:]),
                             inc="DV")
                t2 = em.emit("tensor", lambda E: E.matmul(
                    P[6][:], ctx.c_ones_row[:], linv_b[:], start=True,
                    stop=True), waits=[t1], inc="PE")
                t3 = em.emit("vector", lambda E: E.tensor_copy(linvrep[:],
                                                               P[6][:]),
                             waits=[t2], inc="DV")
                t4 = em.emit("vector", lambda E: E.tensor_tensor(
                    oevac[:], P[2][:], linvrep[:], ALU.mult),
                    waits=[final_tok], inc="DV")
                final_tok = em.dma("gpsimd",
                                   o_dram[h * 128:(h + 1) * 128,
                                          sg * 512:(sg + 1) * 512],
                                   oevac[:], "QSO", waits=[t4])
                t4_prev = t4
            prev_exp_last = em.fence("AC")
            qfree[par] = final_tok
            alfree[par] = pvl_t[ib - 1]
            if (not rank_major and qi == 0 and g + 1 < ngroups
                    and g * heads_per + heads_per < nheads):
                ktok[g + 1] = load_K(g + 1, [grp_done.get(g - 1)])
        grp_done[g] = final_tok
    return final_tok


def _emit_pvl(ctx, pend, exp_t, pvl_t, vfree, pt, vtile, jb):
    em = ctx.em
    P = ctx.P
    sl_, pb, col0, N, kvs, st, sp, tv = pend.pop(jb)
    em.emit("tensor",
            lambda E, sl_=sl_, pb=pb, col0=col0, N=N, kvs=kvs, st=st, sp=sp:
            E.matmul(P[2][:, col0:col0 + N],
                     vtile[sl_][:, kvs * 128:(kvs + 1) * 128],
                     pt[pb][:, col0:col0 + N], start=st, stop=sp),
            waits=[exp_t[jb], tv], inc="PE")
    pvl_t[jb] = em.emit(
        "tensor", lambda E, pb=pb, col0=col0, N=N, st=st, sp=sp:
        E.matmul(P[5][0:1, col0:col0 + N], ctx.c_ones_col[:],
                 pt[pb][:, col0:col0 + N], start=st, stop=sp),
        inc="PE")
    vfree[sl_] = pvl_t[jb]


def build_launchB(layer, debug_o=False):
    nc = bass.Bass("TRN2", target_bir_lowering=False)
    KF = 1024 if layer == 0 else 4096
    aps = {}
    aps["r_in"] = nc.dram_tensor("r_in", [D, TPC], F32, kind="ExternalInput").ap()
    aps["q_in"] = nc.dram_tensor("q_in", [D, TPC], BF16, kind="ExternalInput").ap()
    aps["kg"] = nc.dram_tensor("kg", [KF, S], BF16, kind="ExternalInput").ap()
    aps["vg"] = nc.dram_tensor("vg", [S, KF], BF16, kind="ExternalInput").ap()
    aps["wo_t"] = nc.dram_tensor("wo_t", [32, 128, D], F32, kind="ExternalInput").ap()
    aps["masks"] = nc.dram_tensor("masks", [8, 128, 128], F32,
                                  kind="ExternalInput").ap()
    if layer == 0:
        aps["wf1_t"] = nc.dram_tensor("wf1_t", [32, 128, D], F32,
                                      kind="ExternalInput").ap()
    else:
        aps["albp"] = nc.dram_tensor("albp", [73, 22 * 128], BF16,
                                     kind="ExternalInput").ap()
        aps["alr"] = nc.dram_tensor("alr", [32, 2, 9, 512], BF16,
                                    kind="ExternalInput").ap()
        aps["w12_t"] = nc.dram_tensor("w12_t", [64, 128, D], F32,
                                      kind="ExternalInput").ap()
        aps["w3_t"] = nc.dram_tensor("w3_t", [32, 128, D], F32,
                                     kind="ExternalInput").ap()
        aps["b12"] = nc.dram_tensor("b12", [128, 64], F32, kind="ExternalInput").ap()
        aps["b3"] = nc.dram_tensor("b3", [128, 32], F32, kind="ExternalInput").ap()
    aps["r_out"] = nc.dram_tensor("r_out", [D, TPC], F32,
                                  kind="ExternalOutput").ap()
    aps["r_mid"] = nc.dram_tensor("r_mid", [D, TPC], F32,
                                  kind="ExternalOutput" if debug_o else "Internal")
    aps["o_dram"] = nc.dram_tensor("o_dram", [D, TPC], BF16,
                                   kind="ExternalOutput" if debug_o else "Internal")
    aps["r_mid"] = aps["r_mid"].ap()
    aps["o_dram"] = aps["o_dram"].ap()
    ctx = Ctx(nc, "B%d" % layer)
    if debug_o == "pt":
        ctx.dbg_pt = nc.dram_tensor("dbg_pt", [32 * 128, 512], BF16,
                                    kind="ExternalOutput").ap()
        ctx.dbg_sub = nc.dram_tensor("dbg_sub", [32 * 128, 512], F32,
                                     kind="ExternalOutput").ap()
        ctx.dbg_ol = nc.dram_tensor("dbg_ol", [129, 512], F32,
                                    kind="ExternalOutput").ap()
    emit_phaseB(ctx, layer, aps)
    _final_barrier(ctx.em)
    ctx.em.finalize()
    return nc


def emit_phaseB(ctx, layer, aps, rank_major=False, kv_ready=None,
                entry_wait=None):
    em = ctx.em
    nc = ctx.nc
    for a in range(8):
        tok_masks = em.dma("sync", ctx.WF[:, 2048 + a * 128:2048 + (a + 1) * 128],
                           aps["masks"][a], "QLD", waits=[entry_wait])
    if layer == 1:
        tok_alibi = em.dma("sync", ctx.ALB, aps["albp"][:, :], "QLD")
        ctx.alr_d = aps["alr"]
    else:
        tok_alibi = None

    attn_done = emit_attention(ctx, layer, aps["q_in"], aps["kg"], aps["vg"],
                               aps["o_dram"], tok_masks, tok_alibi,
                               only=getattr(build_launchB, "_only", None),
                               rank_major=rank_major, kv_ready=kv_ready)

    olast = None
    for fc in range(FC):
        olast = em.dma("sync", ctx.B[:, fc * 1024:(fc + 1) * 1024],
                       aps["o_dram"][fc * 128:(fc + 1) * 128, :], "QO",
                       waits=[attn_done] if fc == 0 else None)
    emit_out_proj(ctx, aps["wo_t"], ctx.B, aps["r_in"], aps["r_mid"], None, rhs_ready=olast,
                  init_waits=[attn_done, em.fence("DV")])

    ln2 = emit_ln(ctx, aps["r_mid"], ctx.A,
                  first_waits=[em.fence("QSR0"), em.fence("QSR1"),
                               em.fence("DV")])
    ln2_done = ln2[-1]
    fac = em.fence("AC")

    if layer == 0:
        emit_out_proj(ctx, aps["wf1_t"], ctx.A, aps["r_mid"], aps["r_out"], None, rhs_ready=ln2_done,
                      init_waits=[ln2_done, fac])
    else:
        tb = em.dma("sync", ctx.B12[:], aps["b12"][:, :], "QLD")
        tb3 = em.dma("sync", ctx.B3[:], aps["b3"][:, :], "QLD")
        order = []
        for i in range(32):
            order += [i, 32 + i]
        ws = WStream(ctx, [(aps["w12_t"], oc) for oc in order],
                     init_waits=[ln2_done, fac])
        prev_pair = None
        x4_last = None
        for i in range(32):
            wb, wtok = ws.get(2 * i)
            ew = [ln2_done] if i == 0 else [prev_pair]
            p1tok = emit_mm_oc(ctx, wb, wtok, ctx.A, [ctx.P[0], ctx.P[1]],
                               extra_waits=ew)
            ws.done(2 * i, p1tok)
            wb2, wtok2 = ws.get(2 * i + 1)
            p2tok = emit_mm_oc(ctx, wb2, wtok2, ctx.A, [ctx.P[2], ctx.P[3]],
                               extra_waits=ew)
            ws.done(2 * i + 1, p2tok)
            ts = None
            for h in range(2):
                t1 = em.emit("scalar", lambda E, h=h, i=i:
                             E.activation(out=ctx.STG[:, h * 512:(h + 1) * 512],
                                          in_=ctx.P[h][:], func=AF.Silu,
                                          bias=ctx.B12[:, i:i + 1]),
                             waits=[p1tok, p2tok, tb, prev_pair], inc="AC")
                t2 = em.emit(
                    "vector", lambda E, h=h, i=i:
                    E.tensor_scalar(
                        ctx.STG[:, 1024 + h * 512:1024 + (h + 1) * 512],
                        ctx.P[2 + h][:], ctx.B12[:, 32 + i:32 + i + 1], None,
                        ALU.add),
                    waits=[p2tok, tb], inc="DV")
                ts = em.emit(
                    "vector", lambda E, h=h, i=i:
                    E.tensor_tensor(
                        ctx.B[:, i * 1024 + h * 512:i * 1024 + (h + 1) * 512],
                        ctx.STG[:, h * 512:(h + 1) * 512],
                        ctx.STG[:, 1024 + h * 512:1024 + (h + 1) * 512],
                        ALU.mult),
                    waits=[t1, t2], inc="DV")
            prev_pair = ts
            x4_last = ts
        emit_out_proj(ctx, aps["w3_t"], ctx.B, aps["r_mid"], aps["r_out"], ctx.B3, rhs_ready=x4_last,
                      init_waits=[x4_last, tb3])

    return


def build_fused(debug=False):
    nc = bass.Bass("TRN2", target_bir_lowering=False)
    ein, eout = "ExternalInput", "ExternalOutput"
    dbg = "ExternalOutput" if debug else "Internal"
    r0 = nc.dram_tensor("r_in", [D, TPC], F32, kind=ein).ap()
    wq_t = nc.dram_tensor("wq_t", [32, 128, D], F32, kind=ein).ap()
    wk_t = nc.dram_tensor("wk_t", [8, 128, D], F32, kind=ein).ap()
    wv_t = nc.dram_tensor("wv_t", [8, 128, D], F32, kind=ein).ap()
    ropes = nc.dram_tensor("ropes", [4, 128, TPC], BF16, kind=ein).ap()
    wo0_t = nc.dram_tensor("wo0_t", [32, 128, D], F32, kind=ein).ap()
    masks = nc.dram_tensor("masks", [8, 128, 128], F32, kind=ein).ap()
    wf1_t = nc.dram_tensor("wf1_t", [32, 128, D], F32, kind=ein).ap()
    wqkv_t = nc.dram_tensor("wqkv_t", [96, 128, D], F32, kind=ein).ap()
    lnp = nc.dram_tensor("lnp", [128, 128], F32, kind=ein).ap()
    albp = nc.dram_tensor("albp", [73, 22 * 128], BF16, kind=ein).ap()
    alr = nc.dram_tensor("alr", [32, 2, 9, 512], BF16, kind=ein).ap()
    wo1_t = nc.dram_tensor("wo1_t", [32, 128, D], F32, kind=ein).ap()
    w12_t = nc.dram_tensor("w12_t", [64, 128, D], F32, kind=ein).ap()
    w3_t = nc.dram_tensor("w3_t", [32, 128, D], F32, kind=ein).ap()
    b12 = nc.dram_tensor("b12", [128, 64], F32, kind=ein).ap()
    b3 = nc.dram_tensor("b3", [128, 32], F32, kind=ein).ap()
    y_out = nc.dram_tensor("y_out", [D, TPC], F32, kind=eout).ap()

    q0 = nc.dram_tensor("q0", [D, TPC], BF16)
    k0 = nc.dram_tensor("k0", [1024, TPC], BF16)
    v0 = nc.dram_tensor("v0", [TPC, 1024], BF16)
    agk0 = nc.dram_tensor("agk0", [8 * 1024, TPC], BF16)
    agv0 = nc.dram_tensor("agv0", [8 * TPC, 1024], BF16)
    rm0 = nc.dram_tensor("rm0", [D, TPC], F32)
    o0 = nc.dram_tensor("o0", [D, TPC], BF16)
    r1 = nc.dram_tensor("r1", [D, TPC], F32)
    q1 = nc.dram_tensor("q1", [D, TPC], BF16)
    k1 = nc.dram_tensor("k1", [D, TPC], BF16)
    v1 = nc.dram_tensor("v1", [TPC, D], BF16)
    agk1 = nc.dram_tensor("agk1", [8 * D, TPC], BF16)
    agv1 = nc.dram_tensor("agv1", [8 * TPC, D], BF16)
    rm1 = nc.dram_tensor("rm1", [D, TPC], F32)
    o1 = nc.dram_tensor("o1", [D, TPC], BF16)
    flin = nc.dram_tensor("flin", [1, 4], F32)
    flout = nc.dram_tensor("flout", [1, 4], F32)

    ctx = Ctx(nc, "F")
    em = ctx.em
    RG = [list(range(NCORES))]

    emit_phaseA(ctx, 0, dict(r_in=r0, wq_t=wq_t, wk_t=wk_t, wv_t=wv_t,
                             ropes=ropes, q_out=q0.ap(), k_out=k0.ap(),
                             v_out=v0.ap()))
    em.barrier()
    ck = em.emit("gpsimd", lambda E: E.collective_compute(
        "AllGather", ALU.bypass, replica_groups=RG, ins=[k0.ap()],
        outs=[agk0.ap()]), inc="CC0")
    cv = em.emit("gpsimd", lambda E: E.collective_compute(
        "AllGather", ALU.bypass, replica_groups=RG, ins=[v0.ap()],
        outs=[agv0.ap()]), inc="CC1")
    cf = em.emit("gpsimd", lambda E: E.collective_compute(
        "AllReduce", ALU.add, replica_groups=RG, ins=[flin.ap()],
        outs=[flout.ap()]), inc="CC1")
    ck, cv = cf, cf
    em.barrier()
    emit_phaseB(ctx, 0, dict(r_in=r0, q_in=q0.ap(), kg=agk0.ap(),
                             vg=agv0.ap(), wo_t=wo0_t, masks=masks,
                             wf1_t=wf1_t, r_mid=rm0.ap(), o_dram=o0.ap(),
                             r_out=r1.ap()),
                rank_major=True, kv_ready=[ck, cv])
    em.barrier()
    emit_phaseA(ctx, 1, dict(r_in=r1.ap(), wqkv_t=wqkv_t, lnp=lnp,
                             q_out=q1.ap(), k_out=k1.ap(), v_out=v1.ap()))
    em.barrier()
    ck = em.emit("gpsimd", lambda E: E.collective_compute(
        "AllGather", ALU.bypass, replica_groups=RG, ins=[k1.ap()],
        outs=[agk1.ap()]), inc="CC2")
    cv = em.emit("gpsimd", lambda E: E.collective_compute(
        "AllGather", ALU.bypass, replica_groups=RG, ins=[v1.ap()],
        outs=[agv1.ap()]), inc="CC3")
    cf = em.emit("gpsimd", lambda E: E.collective_compute(
        "AllReduce", ALU.add, replica_groups=RG, ins=[flin.ap()],
        outs=[flout.ap()]), inc="CC3")
    ck, cv = cf, cf
    em.barrier()
    emit_phaseB(ctx, 1, dict(r_in=r1.ap(), q_in=q1.ap(), kg=agk1.ap(),
                             vg=agv1.ap(), wo_t=wo1_t, masks=masks, albp=albp,
                             alr=alr, w12_t=w12_t, w3_t=w3_t, b12=b12, b3=b3,
                             r_mid=rm1.ap(), o_dram=o1.ap(), r_out=y_out),
                rank_major=True, kv_ready=[ck, cv])
    if debug:
        em.barrier()
        for t_, nm in [(q0, "dq0"), (k0, "dk0"), (v0, "dv0"), (agk0, "dagk0"),
                       (agv0, "dagv0"), (rm0, "drm0"), (o0, "do0"), (r1, "dr1"),
                       (q1, "dq1"), (k1, "dk1"), (v1, "dv1"), (rm1, "drm1"),
                       (o1, "do1")]:
            sh = list(t_.shape)
            dt_ = t_.dtype
            dst = nc.dram_tensor(nm, sh, dt_, kind="ExternalOutput").ap()
            em.dma("gpsimd", dst[:, :], t_.ap()[:, :], "QST")
    _final_barrier(em)
    em.finalize()
    return nc



FUSED = True

_programs = {}


def _get_prog(name, builder):
    if name not in _programs:
        _programs[name] = builder()
    return _programs[name]


def kernel(L0_attn_x, L0_attn_wq, L0_attn_wk, L0_attn_wv, L0_attn_wo,
           L0_ffn_identity, L1_attn_w_qkv, L1_attn_q_ln_w, L1_attn_q_ln_b,
           L1_attn_k_ln_w, L1_attn_k_ln_b, L1_attn_w_out, L1_ffn_w12,
           L1_ffn_b12, L1_ffn_w3, L1_ffn_b3):
    cores = list(range(NCORES))
    x = np.asarray(L0_attn_x, dtype=np.float32)[0]
    toks = [tok_ids(c) for c in cores]

    wq_t = tile_w(np.asarray(L0_attn_wq, np.float32))
    wk_t = tile_w(np.asarray(L0_attn_wk, np.float32))
    wv_t = tile_w(np.asarray(L0_attn_wv, np.float32))
    wo_t = tile_w(np.asarray(L0_attn_wo, np.float32))
    id_t = tile_w(np.ascontiguousarray(np.asarray(L0_ffn_identity,
                                                  np.float32).T))
    wqkv_t = tile_w(np.ascontiguousarray(np.asarray(L1_attn_w_qkv,
                                                    np.float32).T))
    wout_t = tile_w(np.ascontiguousarray(np.asarray(L1_attn_w_out,
                                                    np.float32).T))
    w12_t = tile_w(np.ascontiguousarray(np.asarray(L1_ffn_w12, np.float32).T))
    w3_t = tile_w(np.ascontiguousarray(np.asarray(L1_ffn_w3, np.float32).T))
    lnp = np.stack([np.asarray(L1_attn_q_ln_w, np.float32),
                    np.asarray(L1_attn_q_ln_b, np.float32),
                    np.asarray(L1_attn_k_ln_w, np.float32),
                    np.asarray(L1_attn_k_ln_b, np.float32)], axis=0)
    lnp = np.ascontiguousarray(
        lnp.reshape(4, 32, 128).transpose(2, 0, 1).reshape(128, 128))
    b12 = np.ascontiguousarray(np.asarray(L1_ffn_b12, np.float32)
                               .reshape(64, 128).T)
    b3 = np.ascontiguousarray(np.asarray(L1_ffn_b3, np.float32)
                              .reshape(32, 128).T)
    albp = alibi_lhs_packed()
    alrs = [alibi_rhs(c) for c in cores]
    msks = [causal_masks(c) for c in cores]
    ropes = []
    for c in cores:
        qc, qs = rope_tables(c, RS)
        kc_, ks = rope_tables(c, 1.0)
        ropes.append(np.stack([qc, qs, kc_, ks], axis=0))

    r_cur = [np.ascontiguousarray(x[toks[c]].T) for c in cores]

    def gather(shards, feat_major):
        if feat_major:
            KF = shards[0].shape[0]
            out = np.empty((KF, S), dtype=shards[0].dtype)
            for c in cores:
                out[:, toks[c]] = shards[c]
        else:
            KF = shards[0].shape[1]
            out = np.empty((S, KF), dtype=shards[0].dtype)
            for c in cores:
                out[toks[c], :] = shards[c]
        return out

    if FUSED:
        ncF = _get_prog("F", build_fused)
        in_maps = [dict(r_in=r_cur[c], wq_t=wq_t, wk_t=wk_t, wv_t=wv_t,
                        ropes=ropes[c], wo0_t=wo_t, masks=msks[c], wf1_t=id_t,
                        wqkv_t=wqkv_t, lnp=lnp, albp=albp, alr=alrs[c],
                        wo1_t=wout_t, w12_t=w12_t, w3_t=w3_t, b12=b12, b3=b3)
                   for c in cores]
        res = run_bass_kernel_spmd(ncF, in_maps, core_ids=cores)
        y = np.empty((1, S, D), dtype=np.float32)
        for c in cores:
            y[0, toks[c], :] = res.results[c]["y_out"].T
        return y

    for layer in (0, 1):
        ncA = _get_prog(f"A{layer}", lambda layer=layer: build_launchA(layer))
        if layer == 0:
            in_maps = [dict(r_in=r_cur[c], wq_t=wq_t, wk_t=wk_t, wv_t=wv_t,
                            ropes=ropes[c]) for c in cores]
        else:
            in_maps = [dict(r_in=r_cur[c], wqkv_t=wqkv_t, lnp=lnp)
                       for c in cores]
        resA = run_bass_kernel_spmd(ncA, in_maps, core_ids=cores)
        q_sh = [resA.results[c]["q_out"] for c in cores]
        kgf = gather([resA.results[c]["k_out"] for c in cores], True)
        vgf = gather([resA.results[c]["v_out"] for c in cores], False)

        ncB = _get_prog(f"B{layer}", lambda layer=layer: build_launchB(layer))
        if layer == 0:
            in_maps = [dict(r_in=r_cur[c], q_in=q_sh[c], kg=kgf, vg=vgf,
                            wo_t=wo_t, masks=msks[c], wf1_t=id_t)
                       for c in cores]
        else:
            in_maps = [dict(r_in=r_cur[c], q_in=q_sh[c], kg=kgf, vg=vgf,
                            wo_t=wout_t, masks=msks[c], albp=albp, alr=alrs[c],
                            w12_t=w12_t, w3_t=w3_t, b12=b12, b3=b3)
                       for c in cores]
        resB = run_bass_kernel_spmd(ncB, in_maps, core_ids=cores)
        r_cur = [resB.results[c]["r_out"] for c in cores]

    y = np.empty((1, S, D), dtype=np.float32)
    for c in cores:
        y[0, toks[c], :] = r_cur[c].T
    return y


# version 17
# speedup vs baseline: 1.1236x; 1.1236x over previous
import numpy as np
import ml_dtypes
import concourse.bass as bass
import concourse.mybir as mybir
from concourse.bass_utils import run_bass_kernel_spmd

F32 = mybir.dt.float32
BF16 = mybir.dt.bfloat16
AF = mybir.ActivationFunctionType
ALU = mybir.AluOpType
AX = mybir.AxisListType
BF = ml_dtypes.bfloat16

NCORES = 8
D = 4096
S = 8192
TPC = 1024
NSLOT = 8
FC = 32
HD = 128
EPS = 1e-5
RS = float(1.0 / np.sqrt(HD))
NEG = -1.0e9


def tok_ids(c):
    return np.concatenate([np.arange((8 * j + c) * 128, (8 * j + c) * 128 + 128)
                           for j in range(NSLOT)])


def tile_w(W):
    K, O = W.shape
    kc, oc = K // 128, O // 128
    return np.ascontiguousarray(
        W.reshape(kc, 128, oc, 128).transpose(2, 1, 0, 3).reshape(oc, 128, K))


def _llama3_inv_freq():
    THETA, FACTOR, HI_FF, LO_FF, ORIG_MAX = 500000.0, 8.0, 4.0, 1.0, 8192
    inv = 1.0 / THETA ** (np.arange(0, HD, 2, dtype=np.float64) / HD)
    wavelen = 2.0 * np.pi / inv
    low_wl = ORIG_MAX / LO_FF
    high_wl = ORIG_MAX / HI_FF
    smooth = (ORIG_MAX / wavelen - LO_FF) / (HI_FF - LO_FF)
    scaled = np.where(wavelen > low_wl, inv / FACTOR, inv)
    mid = (wavelen <= low_wl) & (wavelen >= high_wl)
    scaled = np.where(mid, (1 - smooth) * inv / FACTOR + smooth * inv, scaled)
    return scaled.astype(np.float32)


def rope_tables(c, scale):
    inv = _llama3_inv_freq().astype(np.float64)
    pos = tok_ids(c).astype(np.float64)
    fr = pos[None, :] * inv[:, None]
    cos = np.cos(fr)
    sin = np.sin(fr)
    cost = (np.concatenate([cos, cos], axis=0) * scale).astype(BF)
    sint = (np.concatenate([-sin, sin], axis=0) * scale).astype(BF)
    return cost, sint


def _alibi_slopes():
    m = np.arange(1, 33, dtype=np.float64) * (8.0 / 32)
    return 1.0 / 2.0 ** m


def _bf16_split3(x):
    h1 = x.astype(BF)
    r = x - h1.astype(np.float64)
    h2 = r.astype(BF)
    h3 = (r - h2.astype(np.float64)).astype(BF)
    return h1, h2, h3


def alibi_lhs_packed():
    out = np.zeros((73, 22 * 128), dtype=BF)
    for b in range(64):
        kg = (b * 128 + np.arange(128)).astype(np.float64)
        kgh, kgl, _ = _bf16_split3(kg)
        pbase = 32 * (b % 3)
        cs = slice((b // 3) * 128, (b // 3) * 128 + 128)
        tile = np.zeros((9, 128), dtype=BF)
        tile[0] = tile[1] = tile[2] = kgh
        tile[3] = tile[4] = tile[5] = kgl
        tile[6] = tile[7] = tile[8] = BF(1.0)
        out[pbase:pbase + 9, cs] = tile
    return out


def alibi_rhs(c):
    sl = _alibi_slopes()
    out = np.zeros((32, 2, 9, 512), dtype=BF)
    for h in range(32):
        sh, sl_, sl2 = _bf16_split3(np.float64(sl[h]))
        for grp in range(2):
            for jj in range(4):
                j = grp * 4 + jj
                qg = ((8 * j + c) * 128 + np.arange(128)).astype(np.float64)
                th, tl, _ = _bf16_split3(sl[h] * qg)
                cs = slice(jj * 128, jj * 128 + 128)
                out[h, grp, 0, cs] = sh
                out[h, grp, 1, cs] = sl_
                out[h, grp, 2, cs] = sl2
                out[h, grp, 3, cs] = sh
                out[h, grp, 4, cs] = sl_
                out[h, grp, 5, cs] = sl2
                out[h, grp, 6, cs] = -th
                out[h, grp, 7, cs] = -tl
    return out


def causal_masks(c):
    out = np.zeros((8, 128, 128), dtype=np.float32)
    tri = np.where(np.arange(128)[:, None] > np.arange(128)[None, :], NEG, 0.0)
    for t in range(8):
        if t == c:
            out[t] = tri.astype(np.float32)
        elif t > c:
            out[t] = NEG
    return out



class Em:
    ENG = ("sync", "tensor", "vector", "scalar", "gpsimd")

    def __init__(self, nc):
        self.nc = nc
        self.q = {e: [] for e in self.ENG}
        self.cnt = {}
        self.handles = {}
        self.lastw = {e: {} for e in self.ENG}

    def sem(self, name):
        if name not in self.handles:
            self.handles[name] = self.nc.alloc_semaphore(name)
            self.cnt[name] = 0
        return name

    def fence(self, name):
        return (name, self.cnt[name]) if self.cnt[name] > 0 else None

    serialize = False

    def emit(self, eng, fn, waits=None, inc=None, amt=1):
        if Em.serialize:
            waits = list(waits or []) + [self.fence(n) for n in self.cnt]
        w = []
        if waits:
            for t in waits:
                if t is None:
                    continue
                s, v = t
                if v > self.lastw[eng].get(s, 0):
                    self.lastw[eng][s] = v
                    w.append((s, v))
        tok = None
        if inc:
            self.cnt[inc] += amt
            tok = (inc, self.cnt[inc])
        self.q[eng].append((w, fn, (inc, amt) if inc else None))
        return tok

    def barrier(self):
        toks = [(n, v) for n, v in self.cnt.items() if v > 0]
        for eng in self.ENG:
            if self.q[eng]:
                self.emit(eng, None, waits=toks)

    def dma(self, eng, out, in_, sem, waits=None):
        return self.emit(eng, lambda E, o=out, i=in_: E.dma_start(out=o, in_=i),
                         waits=waits, inc=sem, amt=16)

    def finalize(self):
        nc = self.nc
        with nc.Block() as block:
            for eng in self.ENG:
                entries = self.q[eng]
                if not entries:
                    continue

                def make(entries=entries):
                    def body(E):
                        for w, fn, inc in entries:
                            for s, v in w:
                                E.wait_ge(self.handles[s], v)
                            ins = fn(E) if fn is not None else None
                            if inc is not None:
                                ins.then_inc(self.handles[inc[0]], inc[1])
                    return body
                getattr(block, eng)(make())



class Ctx:
    def __init__(self, nc, kind):
        self.nc = nc
        self.kind = kind
        em = self.em = Em(nc)
        for s in ("PE", "DV", "AC", "QLD", "QW", "QR0", "QR1", "QV0", "QV1",
                  "QK", "QQ", "QO", "QST", "QSQ", "QSV", "KS0", "KS1", "QSW",
                  "QSO", "QSR0", "QSR1", "QV2", "QV3", "QS2", "QS3",
                  "QV4", "QV5", "QV6", "QV7", "QQ0", "QQ1") + (
                      ("CC0", "CC1", "CC2", "CC3") if kind == "F" else ()):
            em.sem(s)
        A_launch = kind.startswith("A")
        self.A = nc.alloc_sbuf_tensor("arena_a", [128, FC * 1024], BF16)
        self.B = nc.alloc_sbuf_tensor("arena_b", [128, FC * 1024], BF16)
        self.WF = nc.alloc_sbuf_tensor("wf", [128, 4096], F32)
        self.WB = [nc.alloc_sbuf_tensor(f"wb{i}", [128, 4096], BF16)
                   for i in range(2)]
        self.RF = [nc.alloc_sbuf_tensor(f"rf{i}", [128, 1024], F32)
                   for i in range(2)]
        self.STAT = nc.alloc_sbuf_tensor("stat", [1, 2048], F32)
        self.ABR = nc.alloc_sbuf_tensor("abr", [1, 2048], BF16)
        self.AREP = nc.alloc_sbuf_tensor("arep", [128, 2048], BF16)
        if kind == "F":
            SVT = nc.alloc_sbuf_tensor("svt", [128, 4096], BF16)
            self.STG = SVT
            self.M2 = SVT[:, 2048:3072]
            self.VTE = SVT[:, 3072:4096]
            RALB = nc.alloc_sbuf_tensor("ralb", [128, 4096], BF16)
            self.RTAB = RALB
            self.ALB = RALB[0:73, 0:2816]
            self.ALR = RALB[0:73, 2816:3840]
            self.ROUT = [nc.alloc_sbuf_tensor("ro0", [128, 1024], F32)]
            self.GLN = nc.alloc_sbuf_tensor("gln", [128, 128], F32)
            self.B12 = nc.alloc_sbuf_tensor("b12s", [128, 64], F32)
            self.B3 = nc.alloc_sbuf_tensor("b3s", [128, 32], F32)
        if A_launch:
            self.STG = nc.alloc_sbuf_tensor("stg", [128, 3072], BF16)
            VT = nc.alloc_sbuf_tensor("vt", [128, 2048], BF16)
            self.M2 = VT[:, 1024:2048]
            self.VTE = VT[:, 0:1024]
        elif kind != "F":
            self.STG = nc.alloc_sbuf_tensor("stg", [128, 2048], BF16)
            self.ROUT = [nc.alloc_sbuf_tensor(f"ro{i}", [128, 1024], F32)
                         for i in range(2)]
        if kind == "A1":
            self.GLN = nc.alloc_sbuf_tensor("gln", [128, 128], F32)
        if kind == "B1":
            self.ALB = nc.alloc_sbuf_tensor("alb", [73, 22 * 128], BF16)[:, :]
            self.ALR = nc.alloc_sbuf_tensor("alr_sb", [73, 1024], BF16)[:, :]
            self.B12 = nc.alloc_sbuf_tensor("b12s", [128, 64], F32)
            self.B3 = nc.alloc_sbuf_tensor("b3s", [128, 32], F32)
        if kind == "A0":
            self.RTAB = nc.alloc_sbuf_tensor("rtab", [128, 4096], BF16)
        self.FSC = nc.alloc_sbuf_tensor("fsc", [128, 32], F32)
        self.P = [nc.alloc_psum_tensor(f"p{i}", [128, 512], F32) for i in range(7)]
        self.PT = nc.alloc_psum_tensor("ptb", [128, 1024], BF16)
        oc_ = nc.inline_tensor(np.ones((128, 1), dtype=BF), "c_ones_col")
        or_ = nc.inline_tensor(np.ones((1, 128), dtype=BF), "c_ones_row")
        ib_ = nc.inline_tensor(np.eye(128, dtype=np.float32).astype(BF),
                               "c_ident_b")
        if_ = nc.inline_tensor(np.eye(128, dtype=np.float32), "c_ident_f")
        self.c_ones_col = nc.alloc_sbuf_tensor("s_ones_col", [128, 1], BF16)
        self.c_ones_row = nc.alloc_sbuf_tensor("s_ones_row", [1, 128], BF16)
        self.c_ident_b = nc.alloc_sbuf_tensor("s_ident_b", [128, 128], BF16)
        self.c_ident_f = nc.alloc_sbuf_tensor("s_ident_f", [128, 128], F32)
        em.dma("sync", self.c_ones_col[:], oc_.ap(), "QLD")
        em.dma("sync", self.c_ones_row[:], or_.ap(), "QLD")
        em.dma("sync", self.c_ident_b[:], ib_.ap(), "QLD")
        self.tok_consts = em.dma("sync", self.c_ident_f[:], if_.ap(), "QLD")
        self.vt_free = None
        self.rope_m2_free = None



def emit_ln(ctx, r_src, dst_arena, gamma=None, beta=None, src_arena=None,
            out_cb=None, src_toks=None, first_waits=None):
    em = ctx.em
    P = ctx.P
    xb = src_arena if src_arena is not None else ctx.B
    stream = src_arena is None
    SQ = ctx.STG
    cast_toks = [None] * FC
    sq_toks = [None] * FC
    statmm_toks = [None] * FC

    for fc in range(FC):
        sqc = SQ[:, (fc % 2) * 1024:(fc % 2 + 1) * 1024]
        if stream:
            rbuf = ctx.RF[fc % 2]
            w = ([cast_toks[fc - 2], sq_toks[fc - 2]] if fc >= 2
                 else list(first_waits or []))
            tok_ld = em.dma("sync", rbuf[:], r_src[fc * 128:(fc + 1) * 128, :],
                            "QR%d" % (fc % 2), waits=w)
            cast_toks[fc] = em.emit(
                "scalar", lambda E, s=rbuf, d=xb[:, fc * 1024:(fc + 1) * 1024]:
                E.activation(out=d, in_=s[:], func=AF.Copy),
                waits=[tok_ld], inc="AC")
            sq_toks[fc] = em.emit(
                "scalar", lambda E, s=rbuf, d=sqc:
                E.activation(out=d, in_=s[:], func=AF.Square),
                waits=([statmm_toks[fc - 2]] if fc >= 2 else list(first_waits
                                                                 or [])),
                inc="AC")
            src_tok = cast_toks[fc]
        else:
            src_tok = src_toks[fc] if src_toks else None
            xc_ = xb[:, fc * 1024:(fc + 1) * 1024]
            sq_toks[fc] = em.emit(
                "vector", lambda E, x=xc_, d=sqc: E.tensor_tensor(d, x, x,
                                                                  ALU.mult),
                waits=[src_tok] + ([statmm_toks[fc - 2]] if fc >= 2
                                   else list(first_waits or [])),
                inc="DV")
        xc = xb[:, fc * 1024:(fc + 1) * 1024]
        for h in range(2):
            em.emit("tensor",
                    lambda E, r=xc[:, h * 512:(h + 1) * 512], o=P[2 + h][0:1, :],
                    st=(fc == 0), sp=(fc == FC - 1):
                    E.matmul(o, ctx.c_ones_col[:], r, start=st, stop=sp),
                    waits=[src_tok, ctx.tok_consts])
        for h in range(2):
            statmm_toks[fc] = em.emit(
                "tensor",
                lambda E, r=sqc[:, h * 512:(h + 1) * 512], o=P[4 + h][0:1, :],
                st=(fc == 0), sp=(fc == FC - 1):
                E.matmul(o, ctx.c_ones_col[:], r, start=st, stop=sp),
                waits=[sq_toks[fc]], inc="PE" if h == 1 else None)
    stat_done = statmm_toks[FC - 1]

    ST = ctx.STAT
    mu, u, t_ = ST[0:1, 0:512], ST[0:1, 512:1024], ST[0:1, 1024:1536]
    tprev = None
    for h in range(2):
        tk = em.emit("vector", lambda E, h=h: E.tensor_scalar(
            mu[:], P[2 + h][0:1, :], 1.0 / D, None, ALU.mult),
            waits=[stat_done, tprev], inc="DV")
        tk = em.emit("vector", lambda E: E.tensor_tensor(u[:], mu[:], mu[:],
                                                         ALU.mult), inc="DV")
        tk = em.emit("vector", lambda E, h=h: E.tensor_scalar(
            t_[:], P[4 + h][0:1, :], 1.0 / D, None, ALU.mult), inc="DV")
        tk = em.emit("vector", lambda E: E.tensor_tensor(t_[:], t_[:], u[:],
                                                         ALU.subtract), inc="DV")
        tk = em.emit("vector", lambda E: E.tensor_scalar(
            t_[:], t_[:], EPS, None, ALU.add), inc="DV")
        tk = em.emit("scalar", lambda E: E.activation(out=u[:], in_=t_[:],
                                                      func=AF.Ln),
                     waits=[tk], inc="AC")
        tk = em.emit("vector", lambda E: E.tensor_scalar(
            u[:], u[:], -0.5, None, ALU.mult), waits=[tk], inc="DV")
        tk = em.emit("scalar", lambda E: E.activation(out=t_[:], in_=u[:],
                                                      func=AF.Exp),
                     waits=[tk], inc="AC")
        em.emit("scalar", lambda E, h=h: E.activation(
            out=ctx.ABR[:, h * 512:(h + 1) * 512], in_=t_[:], func=AF.Copy),
            inc="AC")
        tk = em.emit("vector", lambda E: E.scalar_tensor_tensor(
            u[:], mu[:], -1.0, t_[:], ALU.mult, ALU.mult), waits=[tk], inc="DV")
        tprev = em.emit("scalar", lambda E, h=h: E.activation(
            out=ctx.ABR[:, 1024 + h * 512:1024 + (h + 1) * 512], in_=u[:],
            func=AF.Copy), waits=[tk], inc="AC")

    ev_toks = []
    for i in range(4):
        tmm = em.emit("tensor", lambda E, i=i:
                      E.matmul(ctx.P[2 + i][:], ctx.c_ones_row[:],
                               ctx.ABR[:, i * 512:(i + 1) * 512],
                               start=True, stop=True),
                      waits=[tprev], inc="PE")
        dst = ctx.AREP[:, i * 512:(i + 1) * 512]
        ev_toks.append(em.emit("scalar", lambda E, s=ctx.P[2 + i], d=dst:
                               E.activation(out=d, in_=s[:], func=AF.Copy),
                               waits=[tmm], inc="AC"))
    arep = ctx.AREP[:, 0:1024]
    brep = ctx.AREP[:, 1024:2048]

    out_toks = []
    stg_busy = [None, None]
    for fc in range(FC):
        xc = xb[:, fc * 1024:(fc + 1) * 1024]
        if dst_arena is not None:
            oc_ = dst_arena[:, fc * 1024:(fc + 1) * 1024]
            wslot = None
        else:
            oc_ = ctx.STG[:, (fc % 2) * 1024:(fc % 2 + 1) * 1024]
            wslot = stg_busy[fc % 2]
        t2 = em.emit("vector", lambda E, x=xc, d=oc_:
                     E.tensor_tensor(d, x, arep, ALU.mult),
                     waits=(ev_toks if fc == 0 else []) +
                     ([wslot] if wslot else []),
                     inc="DV")
        t2 = em.emit("vector", lambda E, d=oc_:
                     E.tensor_tensor(d, d, brep, ALU.add), inc="DV")
        if gamma is not None:
            t2 = em.emit("vector", lambda E, d=oc_, f=fc:
                         E.tensor_scalar(d, d, gamma[:, f:f + 1],
                                         beta[:, f:f + 1], ALU.mult, ALU.add),
                         waits=[t2], inc="DV")
        if out_cb is not None:
            t2 = out_cb(fc, oc_, t2)
            stg_busy[fc % 2] = t2
        out_toks.append(t2)
    return out_toks


class WStream:

    def __init__(self, ctx, seq, init_waits=None):
        self.ctx = ctx
        self.seq = seq
        self.init_waits = list(init_waits or [])
        self.dma_toks = [None] * len(seq)
        self.cast_toks = [None] * len(seq)
        self.free_toks = [None] * len(seq)
        self.issued = 0

    def _issue(self, i):
        ctx, em = self.ctx, self.ctx.em
        wt, oc = self.seq[i]
        w = [self.cast_toks[i - 1]] if i >= 1 else list(self.init_waits)
        self.dma_toks[i] = em.dma("sync", ctx.WF[:], wt[oc], "QW", waits=w)
        cw = [self.dma_toks[i]]
        if i >= 2:
            cw.append(self.free_toks[i - 2])
        self.cast_toks[i] = em.emit(
            "scalar", lambda E, s=ctx.WF, d=ctx.WB[i % 2]:
            E.activation(out=d[:], in_=s[:], func=AF.Copy), waits=cw, inc="AC")
        self.issued = i + 1

    def get(self, i):
        while self.issued <= min(i + 1, len(self.seq) - 1):
            self._issue(self.issued)
        return self.ctx.WB[i % 2], self.cast_toks[i]

    def done(self, i, pe_tok):
        self.free_toks[i] = pe_tok


def emit_mm_oc(ctx, wb, wtok, rhs_arena, psums, extra_waits=None):
    em = ctx.em
    tok = None
    for h in range(2):
        for kc in range(FC):
            w = ([wtok] + (extra_waits or [])) if kc == 0 else None
            tok = em.emit(
                "tensor",
                lambda E, l=wb[:, kc * 128:(kc + 1) * 128],
                r=rhs_arena[:, kc * 1024 + h * 512: kc * 1024 + h * 512 + 512],
                o=psums[h], st=(kc == 0), sp=(kc == FC - 1):
                E.matmul(o[:], l, r, start=st, stop=sp),
                waits=w, inc="PE" if kc == FC - 1 else None)
    return tok


def emit_rope_full(ctx, src, cost, sint, dst, waits):
    em = ctx.em
    m2 = ctx.M2
    ta = em.dma("gpsimd", m2[0:64, :], src[64:128, :], "QSW",
                waits=waits + [ctx.rope_m2_free])
    tb = em.dma("gpsimd", m2[64:128, :], src[0:64, :], "QSW")
    em.emit("vector", lambda E: E.tensor_tensor(dst, src, cost, ALU.mult),
            waits=waits, inc="DV")
    em.emit("vector", lambda E: E.tensor_tensor(m2, m2, sint, ALU.mult),
            waits=[ta, tb], inc="DV")
    tok = em.emit("vector", lambda E: E.tensor_tensor(dst, dst, m2, ALU.add),
                  inc="DV")
    ctx.rope_m2_free = tok
    return tok


def emit_vtrans(ctx, src_view, v_out, kv, waits):
    em = ctx.em
    ttok = None
    for t in range(8):
        ttok = em.emit("tensor", lambda E, t=t:
                       E.transpose(ctx.PT[:, t * 128:(t + 1) * 128],
                                   src_view[:, t * 128:(t + 1) * 128],
                                   ctx.c_ident_b[:]),
                       waits=(waits + [ctx.vt_free]) if t == 0 else None,
                       inc="PE" if t == 7 else None)
    tev = em.emit("scalar", lambda E: E.activation(out=ctx.VTE,
                                                   in_=ctx.PT[:], func=AF.Copy),
                  waits=[ttok], inc="AC")
    td = None
    for t in range(8):
        td = em.dma("gpsimd",
                    v_out[t * 128:(t + 1) * 128, kv * 128:(kv + 1) * 128],
                    ctx.VTE[:, t * 128:(t + 1) * 128], "QSV",
                    waits=[tev] if t == 0 else None)
    ctx.vt_free = td
    return td


def emit_out_proj(ctx, wt, rhs_arena, r_src, r_dst, bias_sb, rhs_ready,
                  init_waits):
    em = ctx.em
    ws = WStream(ctx, [(wt, i) for i in range(32)], init_waits=init_waits)
    evac_dma = [None, None]
    evac_dv = [None, None]
    for i in range(32):
        wb, wtok = ws.get(i)
        p = [ctx.P[(i % 2) * 2], ctx.P[(i % 2) * 2 + 1]]
        ew = [rhs_ready] if i == 0 else [evac_dv[i % 2]]
        ptok = emit_mm_oc(ctx, wb, wtok, rhs_arena, p, extra_waits=ew)
        ws.done(i, ptok)
        rbuf = ctx.RF[i % 2]
        rw = [evac_dv[i % 2]] if i >= 2 else list(init_waits)
        tok_r = em.dma("sync", rbuf[:], r_src[i * 128:(i + 1) * 128, :],
                       "QR%d" % (i % 2), waits=rw)
        nR = len(ctx.ROUT)
        obuf = ctx.ROUT[i % nR]
        tts = []
        for h in range(2):
            if bias_sb is not None:
                tts.append(em.emit(
                    "vector", lambda E, h=h, i=i, o=obuf, r=rbuf, p=p:
                    E.scalar_tensor_tensor(o[:, h * 512:(h + 1) * 512], p[h][:],
                                           bias_sb[:, i:i + 1],
                                           r[:, h * 512:(h + 1) * 512],
                                           ALU.add, ALU.add),
                    waits=[ptok, tok_r, evac_dma[i % nR]], inc="DV"))
            else:
                tts.append(em.emit(
                    "vector", lambda E, h=h, i=i, o=obuf, r=rbuf, p=p:
                    E.tensor_tensor(o[:, h * 512:(h + 1) * 512], p[h][:],
                                    r[:, h * 512:(h + 1) * 512], ALU.add),
                    waits=[ptok, tok_r, evac_dma[i % nR]], inc="DV"))
        td = em.dma("gpsimd", r_dst[i * 128:(i + 1) * 128, :], obuf[:],
                    "QSR%d" % (i % nR), waits=tts)
        evac_dma[i % nR] = td
        evac_dv[i % 2] = tts[-1]



def _clip2(E, dst, p0, p1):
    E.tensor_scalar(dst[:, 0:512], p0[:], 8.0, -8.0, ALU.min, ALU.max)
    return E.tensor_scalar(dst[:, 512:1024], p1[:], 8.0, -8.0, ALU.min, ALU.max)


def build_launchA(layer):
    nc = bass.Bass("TRN2", target_bir_lowering=False)
    aps = {}
    aps["r_in"] = nc.dram_tensor("r_in", [D, TPC], F32, kind="ExternalInput").ap()
    if layer == 0:
        aps["wq_t"] = nc.dram_tensor("wq_t", [32, 128, D], F32, kind="ExternalInput").ap()
        aps["wk_t"] = nc.dram_tensor("wk_t", [8, 128, D], F32, kind="ExternalInput").ap()
        aps["wv_t"] = nc.dram_tensor("wv_t", [8, 128, D], F32, kind="ExternalInput").ap()
        aps["ropes"] = nc.dram_tensor("ropes", [4, 128, TPC], BF16, kind="ExternalInput").ap()
        aps["q_out"] = nc.dram_tensor("q_out", [D, TPC], BF16, kind="ExternalOutput").ap()
        aps["k_out"] = nc.dram_tensor("k_out", [1024, TPC], BF16, kind="ExternalOutput").ap()
        aps["v_out"] = nc.dram_tensor("v_out", [TPC, 1024], BF16, kind="ExternalOutput").ap()
    else:
        aps["wqkv_t"] = nc.dram_tensor("wqkv_t", [96, 128, D], F32, kind="ExternalInput").ap()
        aps["lnp"] = nc.dram_tensor("lnp", [128, 128], F32, kind="ExternalInput").ap()
        aps["q_out"] = nc.dram_tensor("q_out", [D, TPC], BF16, kind="ExternalOutput").ap()
        aps["k_out"] = nc.dram_tensor("k_out", [D, TPC], BF16, kind="ExternalOutput").ap()
        aps["v_out"] = nc.dram_tensor("v_out", [TPC, D], BF16, kind="ExternalOutput").ap()
    ctx = Ctx(nc, "A%d" % layer)
    emit_phaseA(ctx, layer, aps)
    _final_barrier(ctx.em)
    ctx.em.finalize()
    return nc


def emit_phaseA(ctx, layer, aps):
    em = ctx.em
    nc = ctx.nc
    r_in = aps["r_in"]
    q_out, k_out, v_out = aps["q_out"], aps["k_out"], aps["v_out"]

    if layer == 0:
        for a in range(4):
            tok_rt = em.dma("sync", ctx.RTAB[:, a * 1024:(a + 1) * 1024],
                            aps["ropes"][a], "QLD")
    else:
        tok_ln = em.dma("sync", ctx.GLN[:], aps["lnp"][:, :], "QLD")
        tok_ln = em.emit("vector", lambda E: E.tensor_scalar(
            ctx.GLN[:, 64:128], ctx.GLN[:, 64:128], RS, None, ALU.mult),
            waits=[tok_ln], inc="DV")

    ln_toks = emit_ln(ctx, r_in, ctx.A)
    ln_done = ln_toks[-1]
    fac0 = em.fence("AC")

    if layer == 0:
        ws = WStream(ctx, [(aps["wq_t"], i) for i in range(32)]
                     + [(aps["wk_t"], i) for i in range(8)]
                     + [(aps["wv_t"], i) for i in range(8)])
        prev_done = None
        for i in range(48):
            wb, wtok = ws.get(i)
            ew = [ln_done, fac0] if i == 0 else [prev_done]
            ptok = emit_mm_oc(ctx, wb, wtok, ctx.A, [ctx.P[0], ctx.P[1]],
                              extra_waits=ew)
            ws.done(i, ptok)
            if i < 40:
                tv = None
                for h in range(2):
                    tv = em.emit(
                        "scalar", lambda E, h=h:
                        E.activation(out=ctx.STG[:, h * 512:(h + 1) * 512],
                                     in_=ctx.P[h][:], func=AF.Copy),
                        waits=[ptok], inc="AC")
                if i < 32:
                    cost, sint = ctx.RTAB[:, 0:1024], ctx.RTAB[:, 1024:2048]
                    dst_dram = q_out[i * 128:(i + 1) * 128, :]
                else:
                    cost, sint = ctx.RTAB[:, 2048:3072], ctx.RTAB[:, 3072:4096]
                    dst_dram = k_out[(i - 32) * 128:(i - 31) * 128, :]
                tr = emit_rope_full(ctx, ctx.STG[:, 0:1024], cost, sint,
                                    ctx.STG[:, 1024:2048], [tv, tok_rt])
                prev_done = em.dma("gpsimd", dst_dram, ctx.STG[:, 1024:2048],
                                   "QSQ", waits=[tr])
            else:
                kv = i - 40
                tv = None
                for h in range(2):
                    tv = em.emit(
                        "scalar", lambda E, h=h:
                        E.activation(
                            out=ctx.STG[:, 2048 + h * 512:2048 + (h + 1) * 512],
                            in_=ctx.P[h][:], func=AF.Copy),
                        waits=[ptok], inc="AC")
                prev_done = emit_vtrans(ctx, ctx.STG[:, 2048:3072], v_out,
                                        kv, [tv])
    else:
        ws = WStream(ctx, [(aps["wqkv_t"], oc) for oc in
                           list(range(32, 96)) + list(range(0, 32))])
        prev_done = None
        k_toks = [None] * FC
        q_toks = [None] * FC
        b_free = [None] * FC
        for i in range(96):
            wb, wtok = ws.get(i)
            ew = [ln_done, fac0] if i == 0 else [prev_done]
            ptok = emit_mm_oc(ctx, wb, wtok, ctx.A, [ctx.P[0], ctx.P[1]],
                              extra_waits=ew)
            ws.done(i, ptok)
            oc = ws.seq[i][1]
            if 32 <= oc < 64:
                fc = oc - 32
                tcl = em.emit("vector",
                              lambda E, d=ctx.B[:, fc * 1024:(fc + 1) * 1024]:
                              _clip2(E, d, ctx.P[0], ctx.P[1]),
                              waits=[ptok], inc="DV", amt=2)
                k_toks[fc] = tcl
                prev_done = tcl
            elif oc >= 64:
                kv = oc - 64
                tcl = em.emit("vector", lambda E, d=ctx.STG[:, 2048:3072]:
                              _clip2(E, d, ctx.P[0], ctx.P[1]),
                              waits=[ptok, ctx.vt_free], inc="DV", amt=2)
                prev_done = emit_vtrans(ctx, ctx.STG[:, 2048:3072], v_out,
                                        kv, [tcl])
                if kv == 0:
                    def k_cb(fc2, view, tok2):
                        t = em.dma("gpsimd", k_out[fc2 * 128:(fc2 + 1) * 128, :],
                                   view, "KS%d" % (fc2 % 2), waits=[tok2])
                        b_free[fc2] = t
                        return t
                    emit_ln(ctx, None, None, gamma=ctx.GLN[:, 64:96],
                            beta=ctx.GLN[:, 96:128], src_arena=ctx.B,
                            out_cb=k_cb, src_toks=k_toks, first_waits=[tok_ln])
            else:
                fc = oc
                tcl = em.emit("vector",
                              lambda E, d=ctx.B[:, fc * 1024:(fc + 1) * 1024]:
                              _clip2(E, d, ctx.P[0], ctx.P[1]),
                              waits=[ptok, b_free[fc]], inc="DV", amt=2)
                q_toks[fc] = tcl
                prev_done = tcl

        def q_cb(fc2, view, tok2):
            return em.dma("gpsimd", q_out[fc2 * 128:(fc2 + 1) * 128, :],
                          view, "QS%d" % (2 + fc2 % 2), waits=[tok2])
        emit_ln(ctx, None, None, gamma=ctx.GLN[:, 0:32], beta=ctx.GLN[:, 32:64],
                src_arena=ctx.B, out_cb=q_cb, src_toks=q_toks,
                first_waits=[prev_done])

    return


def _final_barrier(em):
    sems = ["QST", "QSQ", "QSV", "KS0", "KS1", "QSW", "QSO", "QSR0", "QSR1",
            "QS2", "QS3"]
    em.emit("gpsimd", None, waits=[(n, em.cnt[n]) for n in sems
                                   if em.cnt[n] > 0])



def emit_attention(ctx, layer, q_in, kg, vg, o_dram, tok_masks, tok_alibi,
                   only=None, rank_major=False, kv_ready=None):
    em = ctx.em
    P = ctx.P
    KF = 1024 if layer == 0 else 4096
    if layer == 0:
        ngroups, heads_per = 4, 8
        kvslot_of = lambda qi: qi // 4
    else:
        ngroups, heads_per = 16, 2
        kvslot_of = lambda qi: qi

    WF = ctx.WF
    DEPTH = 4
    sub = [WF[:, i * 512:(i + 1) * 512] for i in range(4)]
    mrep = WF[:, 3072:3584]
    linvrep = WF[:, 3584:4096]
    MASKS = WF[:, 2048:3072]
    qcur2 = [ctx.B[:, 0:1024], ctx.B[:, 1024:2048]]
    NV = 8
    vtile = [ctx.B[:, 2048 + i * 256: 2048 + (i + 1) * 256] for i in range(NV)]
    pt = [ctx.B[:, 6144 + i * 512: 6144 + (i + 1) * 512] for i in range(4)]
    oevac = ctx.B[:, 8192:8704]
    SP = [ctx.P[0], ctx.P[1], ctx.P[3], ctx.P[4]]
    P1R = [ctx.P[0], ctx.P[1], ctx.P[3], ctx.P[4], ctx.P[2], ctx.P[5], ctx.P[6]]
    mnegc = ctx.FSC[:, 8:16]
    PCOL = ctx.FSC[:, 16:32]
    mrow_b = ctx.ABR[0:1, 0:1024]
    linv_f = ctx.STAT[0:1, 1536:2048]
    linv_b = ctx.ABR[0:1, 1024:1536]
    if layer == 1:
        ALRs = [ctx.ALR, ctx.B[0:73, 8704:9728]]

    def load_K(g, waits):
        base = (g % 2) * 16384
        tk = None
        for s_ in range(2):
            kvh = 2 * g + s_
            dst0 = base + s_ * 8192
            if not rank_major:
                tk = em.dma("sync", ctx.A[:, dst0:dst0 + 8192],
                            kg[kvh * 128:(kvh + 1) * 128, :], "QK",
                            waits=waits if s_ == 0 else None)
            else:
                for b_ in range(64):
                    tk = em.dma(
                        "sync",
                        ctx.A[:, dst0 + b_ * 128:dst0 + (b_ + 1) * 128],
                        kg[(b_ % 8) * KF + kvh * 128:
                           (b_ % 8) * KF + (kvh + 1) * 128,
                           (b_ // 8) * 128:(b_ // 8 + 1) * 128],
                        "QK", waits=waits if (s_ == 0 and b_ == 0) else None)
        return tk

    vfree = {i: None for i in range(NV)}
    qfree = {0: None, 1: None}
    alfree = {0: None, 1: None}
    grp_done = {}
    ktok = {0: load_K(0, [kv_ready[0]] if kv_ready else None)}
    kpend = {}

    def stage_K(g, waits):
        base = (g % 2) * 16384
        pairs = []
        for s_ in range(2):
            kvh = 2 * g + s_
            dst0 = base + s_ * 8192
            for b_ in range(64):
                pairs.append((dst0 + b_ * 128,
                              (b_ % 8) * KF + kvh * 128, (b_ // 8) * 128))
        kpend[g] = [pairs, waits]

    def issue_k(n=1):
        for g_ in sorted(kpend):
            pairs, w = kpend[g_]
            for _ in range(n):
                if not pairs:
                    break
                d0, r0, c0 = pairs.pop(0)
                ktok[g_] = em.dma("sync", ctx.A[:, d0:d0 + 128],
                                  kg[r0:r0 + 128, c0:c0 + 128], "QK",
                                  waits=w)
                kpend[g_][1] = None
                w = None
            if not pairs:
                del kpend[g_]
            return
    t4_prev = None
    prev_exp_last = None
    final_tok = None
    nheads = min(only, ngroups * heads_per) if only else ngroups * heads_per

    for g in range(ngroups):
        while g in kpend:
            issue_k(8)
        for qi in range(heads_per):
            if (qi == 1 and rank_major and g + 1 < ngroups
                    and (g + 1) * heads_per < nheads):
                stage_K(g + 1, [grp_done.get(g - 1)])
            h = g * heads_per + qi
            if h >= nheads:
                break
            kvs = kvslot_of(qi)
            base = (g % 2) * 16384
            kc = ctx.A[:, base + kvs * 8192: base + (kvs + 1) * 8192]
            par = h % 2
            qcur = qcur2[par]
            tok_q = em.dma("sync", qcur, q_in[h * 128:(h + 1) * 128, :],
                           "QQ%d" % par, waits=[qfree[par]])
            tok_k = ktok[g]
            if layer == 1:
                AL = ALRs[par]
                tok_ar = None
                for pb_ in (0, 32, 64):
                    for g_ in (0, 1):
                        tok_ar = em.dma("sync",
                                        AL[pb_:pb_ + 9,
                                           g_ * 512:(g_ + 1) * 512],
                                        ctx.alr_d[h, g_], "QLD",
                                        waits=[alfree[par]])
            else:
                AL = None
                tok_ar = None

            red = {}
            i_mm = 0
            for j in range(NSLOT):
                for kap in range(2 * (j + 1)):
                    b2 = i_mm % 7
                    w = [tok_q, tok_k]
                    if i_mm >= 7:
                        w.append(red[i_mm - 7])
                    else:
                        w += [t4_prev, prev_exp_last]
                    tmm = em.emit(
                        "tensor", lambda E, j=j, kap=kap, b2=b2, kc=kc,
                        qcur=qcur:
                        E.matmul(P1R[b2][:], qcur[:, j * 128:(j + 1) * 128],
                                 kc[:, kap * 512:(kap + 1) * 512],
                                 start=True, stop=True),
                        waits=w, inc="PE")
                    red[i_mm] = em.emit(
                        "vector", lambda E, b2=b2, kap=kap:
                        E.tensor_reduce(PCOL[:, kap:kap + 1], P1R[b2][:],
                                        AX.X, ALU.max),
                        waits=[tmm], inc="DV")
                    i_mm += 1
                em.emit(
                    "vector", lambda E, j=j, n=2 * (j + 1):
                    E.tensor_reduce(mnegc[:, j:j + 1], PCOL[:, 0:n],
                                    AX.X, ALU.max, negate=True),
                    inc="DV")
            tneg = em.fence("DV")

            rows_t = [None] * NSLOT
            TPP = [P[6], P[0]]
            for j in range(NSLOT):
                tt = em.emit("tensor", lambda E, j=j:
                             E.transpose(TPP[j % 2][0:1, 0:128],
                                         mnegc[:, j:j + 1], ctx.c_ident_f[:]),
                             waits=[tneg] + ([rows_t[j - 2]] if j >= 2 else []),
                             inc="PE")
                rows_t[j] = em.emit(
                    "vector", lambda E, j=j:
                    E.tensor_copy(mrow_b[:, j * 128:(j + 1) * 128],
                                  TPP[j % 2][0:1, 0:128]),
                    waits=[tt], inc="DV")
            trow = rows_t[-1]

            if layer == 1:
                tok_mr = None
                for pb_ in (0, 32, 64):
                    tok_mr = em.dma("gpsimd", AL[pb_ + 8:pb_ + 9, 0:1024],
                                    mrow_b[:, 0:1024], "QSW", waits=[trow])
            else:
                tok_mr = None

            for sg in range(2):
                jbase = 4 * sg
                maxkap = 2 * (jbase + 4)
                if layer == 0:
                    tb = em.emit("tensor", lambda E, sg=sg:
                                 E.matmul(P[6][:], ctx.c_ones_row[:],
                                          mrow_b[:, sg * 512:(sg + 1) * 512],
                                          start=True, stop=True),
                                 waits=[trow, t4_prev], inc="PE")
                    tmrep = em.emit("vector",
                                    lambda E: E.tensor_copy(mrep[:], P[6][:]),
                                    waits=[tb], inc="DV")
                else:
                    tmrep = None
                insts = []
                for kap in range(maxkap):
                    ja = max(jbase, (kap + 2) // 2 - 1)
                    col0 = (ja - jbase) * 128
                    N = 512 - col0
                    for bb in range(4):
                        insts.append((kap, bb, ja, col0, N))
                n_inst = len(insts)
                sub_t = {}
                exp_t = {}
                pvl_t = {}
                pend = {}
                vtok = {}
                LAG = 3
                LOOK = 4

                def vdma(i_):
                    kap, bb, ja, col0, N = insts[i_]
                    b = 4 * kap + bb
                    sl_ = i_ % NV
                    if not rank_major:
                        vsrc = vg[b * 128:(b + 1) * 128,
                                  g * 256:(g + 1) * 256]
                    else:
                        r0_ = (b % 8) * TPC + (b // 8) * 128
                        vsrc = vg[r0_:r0_ + 128, g * 256:(g + 1) * 256]
                    vtok[i_] = em.dma("sync", vtile[sl_][:], vsrc,
                                      "QV%d" % sl_,
                                      waits=[vfree[sl_]] +
                                      ([kv_ready[1]] if kv_ready else []))

                for i_ in range(min(LOOK, n_inst)):
                    vdma(i_)
                ib = 0
                for kap, bb, ja, col0, N in insts:
                    b = 4 * kap + bb
                    pb = ib % DEPTH
                    w = [tok_q, tok_k, tok_ar, tok_alibi]
                    if ib >= DEPTH:
                        w.append(exp_t[ib - DEPTH] if layer == 1
                                 else sub_t[ib - DEPTH])
                    else:
                        w += [trow, t4_prev, tok_mr, tmrep]
                    tmm = em.emit(
                        "tensor", lambda E, b=b, pb=pb, col0=col0, N=N,
                        jbase=jbase, kc=kc, sp_=(layer == 0), qcur=qcur:
                        E.matmul(SP[pb][:, col0:col0 + N],
                                 kc[:, b * 128:(b + 1) * 128],
                                 qcur[:, jbase * 128 + col0:
                                      jbase * 128 + col0 + N],
                                 start=True, stop=sp_),
                        waits=w, inc="PE")
                    if layer == 1:
                        pbase = 32 * (b % 3)
                        cs = (b // 3) * 128
                        tmm = em.emit(
                            "tensor", lambda E, pb=pb, col0=col0, N=N,
                            sg=sg, pbase=pbase, cs=cs, AL=AL:
                            E.matmul(SP[pb][:, col0:col0 + N],
                                     ctx.ALB[pbase:pbase + 9, cs:cs + 128],
                                     AL[pbase:pbase + 9,
                                        sg * 512 + col0:sg * 512 + col0 + N],
                                     start=False, stop=True),
                            inc="PE")
                        tsb = tmm
                        jd = b // 8
                        if ja <= jd < jbase + 4:
                            t_ = b % 8
                            cj = (jd - jbase) * 128
                            tsb = em.emit(
                                "vector", lambda E, pb=pb, cj=cj, t_=t_:
                                E.tensor_tensor(
                                    SP[pb][:, cj:cj + 128],
                                    SP[pb][:, cj:cj + 128],
                                    MASKS[:, t_ * 128:(t_ + 1) * 128],
                                    ALU.add),
                                waits=[tok_masks, tmm], inc="DV")
                        sub_t[ib] = tsb
                        exp_t[ib] = em.emit(
                            "scalar", lambda E, pb=pb, col0=col0, N=N:
                            E.activation(out=pt[pb][:, col0:col0 + N],
                                         in_=SP[pb][:, col0:col0 + N],
                                         func=AF.Exp),
                            waits=[tsb] + ([pvl_t[ib - DEPTH]]
                                           if ib >= DEPTH else []),
                            inc="AC")
                    else:
                        tsb = em.emit(
                            "vector", lambda E, pb=pb, col0=col0, N=N:
                            E.tensor_tensor(sub[pb][:, col0:col0 + N],
                                            SP[pb][:, col0:col0 + N],
                                            mrep[:, col0:col0 + N], ALU.add),
                            waits=[tmm, tmrep] +
                            ([exp_t[ib - DEPTH]] if ib >= DEPTH else []),
                            inc="DV")
                        jd = b // 8
                        if ja <= jd < jbase + 4:
                            t_ = b % 8
                            cj = (jd - jbase) * 128
                            tsb = em.emit(
                                "vector", lambda E, pb=pb, cj=cj, t_=t_:
                                E.tensor_tensor(
                                    sub[pb][:, cj:cj + 128],
                                    sub[pb][:, cj:cj + 128],
                                    MASKS[:, t_ * 128:(t_ + 1) * 128],
                                    ALU.add),
                                waits=[tok_masks], inc="DV")
                        sub_t[ib] = tsb
                        exp_t[ib] = em.emit(
                            "scalar", lambda E, pb=pb, col0=col0, N=N:
                            E.activation(out=pt[pb][:, col0:col0 + N],
                                         in_=sub[pb][:, col0:col0 + N],
                                         func=AF.Exp),
                            waits=[tsb] + ([pvl_t[ib - DEPTH]]
                                           if ib >= DEPTH else []),
                            inc="AC")
                    pend[ib] = (ib % NV, pb, col0, N, kvs,
                                (kap == 0 and bb == 0),
                                (kap == maxkap - 1 and bb == 3), vtok[ib])
                    if ib + LOOK < n_inst:
                        vdma(ib + LOOK)
                    if rank_major:
                        issue_k(1)
                    if ib - LAG >= 0:
                        _emit_pvl(ctx, pend, exp_t, pvl_t, vfree, pt,
                                  vtile, ib - LAG)
                    ib += 1
                for jb in range(max(0, ib - LAG), ib):
                    _emit_pvl(ctx, pend, exp_t, pvl_t, vfree, pt, vtile, jb)
                t1 = em.emit("vector", lambda E: E.reciprocal(linv_f[:],
                                                              P[5][0:1, :]),
                             waits=[pvl_t[ib - 1]], inc="DV")
                t1 = em.emit("vector", lambda E: E.tensor_copy(linv_b[:],
                                                               linv_f[:]),
                             inc="DV")
                t2 = em.emit("tensor", lambda E: E.matmul(
                    P[6][:], ctx.c_ones_row[:], linv_b[:], start=True,
                    stop=True), waits=[t1], inc="PE")
                t3 = em.emit("vector", lambda E: E.tensor_copy(linvrep[:],
                                                               P[6][:]),
                             waits=[t2], inc="DV")
                t4 = em.emit("vector", lambda E: E.tensor_tensor(
                    oevac[:], P[2][:], linvrep[:], ALU.mult),
                    waits=[final_tok], inc="DV")
                final_tok = em.dma("gpsimd",
                                   o_dram[h * 128:(h + 1) * 128,
                                          sg * 512:(sg + 1) * 512],
                                   oevac[:], "QSO", waits=[t4])
                t4_prev = t4
            prev_exp_last = em.fence("AC")
            qfree[par] = final_tok
            alfree[par] = pvl_t[ib - 1]
            if (not rank_major and qi == 0 and g + 1 < ngroups
                    and g * heads_per + heads_per < nheads):
                ktok[g + 1] = load_K(g + 1, [grp_done.get(g - 1)])
        grp_done[g] = final_tok
    return final_tok


def _emit_pvl(ctx, pend, exp_t, pvl_t, vfree, pt, vtile, jb):
    em = ctx.em
    P = ctx.P
    sl_, pb, col0, N, kvs, st, sp, tv = pend.pop(jb)
    em.emit("tensor",
            lambda E, sl_=sl_, pb=pb, col0=col0, N=N, kvs=kvs, st=st, sp=sp:
            E.matmul(P[2][:, col0:col0 + N],
                     vtile[sl_][:, kvs * 128:(kvs + 1) * 128],
                     pt[pb][:, col0:col0 + N], start=st, stop=sp),
            waits=[exp_t[jb], tv], inc="PE")
    pvl_t[jb] = em.emit(
        "tensor", lambda E, pb=pb, col0=col0, N=N, st=st, sp=sp:
        E.matmul(P[5][0:1, col0:col0 + N], ctx.c_ones_col[:],
                 pt[pb][:, col0:col0 + N], start=st, stop=sp),
        inc="PE")
    vfree[sl_] = pvl_t[jb]


def build_launchB(layer, debug_o=False):
    nc = bass.Bass("TRN2", target_bir_lowering=False)
    KF = 1024 if layer == 0 else 4096
    aps = {}
    aps["r_in"] = nc.dram_tensor("r_in", [D, TPC], F32, kind="ExternalInput").ap()
    aps["q_in"] = nc.dram_tensor("q_in", [D, TPC], BF16, kind="ExternalInput").ap()
    aps["kg"] = nc.dram_tensor("kg", [KF, S], BF16, kind="ExternalInput").ap()
    aps["vg"] = nc.dram_tensor("vg", [S, KF], BF16, kind="ExternalInput").ap()
    aps["wo_t"] = nc.dram_tensor("wo_t", [32, 128, D], F32, kind="ExternalInput").ap()
    aps["masks"] = nc.dram_tensor("masks", [8, 128, 128], F32,
                                  kind="ExternalInput").ap()
    if layer == 0:
        aps["wf1_t"] = nc.dram_tensor("wf1_t", [32, 128, D], F32,
                                      kind="ExternalInput").ap()
    else:
        aps["albp"] = nc.dram_tensor("albp", [73, 22 * 128], BF16,
                                     kind="ExternalInput").ap()
        aps["alr"] = nc.dram_tensor("alr", [32, 2, 9, 512], BF16,
                                    kind="ExternalInput").ap()
        aps["w12_t"] = nc.dram_tensor("w12_t", [64, 128, D], F32,
                                      kind="ExternalInput").ap()
        aps["w3_t"] = nc.dram_tensor("w3_t", [32, 128, D], F32,
                                     kind="ExternalInput").ap()
        aps["b12"] = nc.dram_tensor("b12", [128, 64], F32, kind="ExternalInput").ap()
        aps["b3"] = nc.dram_tensor("b3", [128, 32], F32, kind="ExternalInput").ap()
    aps["r_out"] = nc.dram_tensor("r_out", [D, TPC], F32,
                                  kind="ExternalOutput").ap()
    aps["r_mid"] = nc.dram_tensor("r_mid", [D, TPC], F32,
                                  kind="ExternalOutput" if debug_o else "Internal")
    aps["o_dram"] = nc.dram_tensor("o_dram", [D, TPC], BF16,
                                   kind="ExternalOutput" if debug_o else "Internal")
    aps["r_mid"] = aps["r_mid"].ap()
    aps["o_dram"] = aps["o_dram"].ap()
    ctx = Ctx(nc, "B%d" % layer)
    if debug_o == "pt":
        ctx.dbg_pt = nc.dram_tensor("dbg_pt", [32 * 128, 512], BF16,
                                    kind="ExternalOutput").ap()
        ctx.dbg_sub = nc.dram_tensor("dbg_sub", [32 * 128, 512], F32,
                                     kind="ExternalOutput").ap()
        ctx.dbg_ol = nc.dram_tensor("dbg_ol", [129, 512], F32,
                                    kind="ExternalOutput").ap()
    emit_phaseB(ctx, layer, aps)
    _final_barrier(ctx.em)
    ctx.em.finalize()
    return nc


def emit_phaseB(ctx, layer, aps, rank_major=False, kv_ready=None,
                entry_wait=None):
    em = ctx.em
    nc = ctx.nc
    for a in range(8):
        tok_masks = em.dma("sync", ctx.WF[:, 2048 + a * 128:2048 + (a + 1) * 128],
                           aps["masks"][a], "QLD", waits=[entry_wait])
    if layer == 1:
        tok_alibi = em.dma("sync", ctx.ALB, aps["albp"][:, :], "QLD")
        ctx.alr_d = aps["alr"]
    else:
        tok_alibi = None

    attn_done = emit_attention(ctx, layer, aps["q_in"], aps["kg"], aps["vg"],
                               aps["o_dram"], tok_masks, tok_alibi,
                               only=getattr(build_launchB, "_only", None),
                               rank_major=rank_major, kv_ready=kv_ready)

    olast = None
    for fc in range(FC):
        olast = em.dma("sync", ctx.B[:, fc * 1024:(fc + 1) * 1024],
                       aps["o_dram"][fc * 128:(fc + 1) * 128, :], "QO",
                       waits=[attn_done] if fc == 0 else None)
    emit_out_proj(ctx, aps["wo_t"], ctx.B, aps["r_in"], aps["r_mid"], None, rhs_ready=olast,
                  init_waits=[attn_done, em.fence("DV")])

    ln2 = emit_ln(ctx, aps["r_mid"], ctx.A,
                  first_waits=[em.fence("QSR0"), em.fence("QSR1"),
                               em.fence("DV")])
    ln2_done = ln2[-1]
    fac = em.fence("AC")

    if layer == 0:
        emit_out_proj(ctx, aps["wf1_t"], ctx.A, aps["r_mid"], aps["r_out"], None, rhs_ready=ln2_done,
                      init_waits=[ln2_done, fac])
    else:
        tb = em.dma("sync", ctx.B12[:], aps["b12"][:, :], "QLD")
        tb3 = em.dma("sync", ctx.B3[:], aps["b3"][:, :], "QLD")
        order = []
        for i in range(32):
            order += [i, 32 + i]
        ws = WStream(ctx, [(aps["w12_t"], oc) for oc in order],
                     init_waits=[ln2_done, fac])
        prev_pair = None
        x4_last = None
        for i in range(32):
            wb, wtok = ws.get(2 * i)
            ew = [ln2_done] if i == 0 else [prev_pair]
            p1tok = emit_mm_oc(ctx, wb, wtok, ctx.A, [ctx.P[0], ctx.P[1]],
                               extra_waits=ew)
            ws.done(2 * i, p1tok)
            wb2, wtok2 = ws.get(2 * i + 1)
            p2tok = emit_mm_oc(ctx, wb2, wtok2, ctx.A, [ctx.P[2], ctx.P[3]],
                               extra_waits=ew)
            ws.done(2 * i + 1, p2tok)
            ts = None
            for h in range(2):
                t1 = em.emit("scalar", lambda E, h=h, i=i:
                             E.activation(out=ctx.STG[:, h * 512:(h + 1) * 512],
                                          in_=ctx.P[h][:], func=AF.Silu,
                                          bias=ctx.B12[:, i:i + 1]),
                             waits=[p1tok, p2tok, tb, prev_pair], inc="AC")
                t2 = em.emit(
                    "vector", lambda E, h=h, i=i:
                    E.tensor_scalar(
                        ctx.STG[:, 1024 + h * 512:1024 + (h + 1) * 512],
                        ctx.P[2 + h][:], ctx.B12[:, 32 + i:32 + i + 1], None,
                        ALU.add),
                    waits=[p2tok, tb], inc="DV")
                ts = em.emit(
                    "vector", lambda E, h=h, i=i:
                    E.tensor_tensor(
                        ctx.B[:, i * 1024 + h * 512:i * 1024 + (h + 1) * 512],
                        ctx.STG[:, h * 512:(h + 1) * 512],
                        ctx.STG[:, 1024 + h * 512:1024 + (h + 1) * 512],
                        ALU.mult),
                    waits=[t1, t2], inc="DV")
            prev_pair = ts
            x4_last = ts
        emit_out_proj(ctx, aps["w3_t"], ctx.B, aps["r_mid"], aps["r_out"], ctx.B3, rhs_ready=x4_last,
                      init_waits=[x4_last, tb3])

    return


def build_fused(debug=False):
    nc = bass.Bass("TRN2", target_bir_lowering=False)
    ein, eout = "ExternalInput", "ExternalOutput"
    dbg = "ExternalOutput" if debug else "Internal"
    r0 = nc.dram_tensor("r_in", [D, TPC], F32, kind=ein).ap()
    wq_t = nc.dram_tensor("wq_t", [32, 128, D], F32, kind=ein).ap()
    wk_t = nc.dram_tensor("wk_t", [8, 128, D], F32, kind=ein).ap()
    wv_t = nc.dram_tensor("wv_t", [8, 128, D], F32, kind=ein).ap()
    ropes = nc.dram_tensor("ropes", [4, 128, TPC], BF16, kind=ein).ap()
    wo0_t = nc.dram_tensor("wo0_t", [32, 128, D], F32, kind=ein).ap()
    masks = nc.dram_tensor("masks", [8, 128, 128], F32, kind=ein).ap()
    wf1_t = nc.dram_tensor("wf1_t", [32, 128, D], F32, kind=ein).ap()
    wqkv_t = nc.dram_tensor("wqkv_t", [96, 128, D], F32, kind=ein).ap()
    lnp = nc.dram_tensor("lnp", [128, 128], F32, kind=ein).ap()
    albp = nc.dram_tensor("albp", [73, 22 * 128], BF16, kind=ein).ap()
    alr = nc.dram_tensor("alr", [32, 2, 9, 512], BF16, kind=ein).ap()
    wo1_t = nc.dram_tensor("wo1_t", [32, 128, D], F32, kind=ein).ap()
    w12_t = nc.dram_tensor("w12_t", [64, 128, D], F32, kind=ein).ap()
    w3_t = nc.dram_tensor("w3_t", [32, 128, D], F32, kind=ein).ap()
    b12 = nc.dram_tensor("b12", [128, 64], F32, kind=ein).ap()
    b3 = nc.dram_tensor("b3", [128, 32], F32, kind=ein).ap()
    y_out = nc.dram_tensor("y_out", [D, TPC], F32, kind=eout).ap()

    q0 = nc.dram_tensor("q0", [D, TPC], BF16)
    k0 = nc.dram_tensor("k0", [1024, TPC], BF16)
    v0 = nc.dram_tensor("v0", [TPC, 1024], BF16)
    agk0 = nc.dram_tensor("agk0", [8 * 1024, TPC], BF16)
    agv0 = nc.dram_tensor("agv0", [8 * TPC, 1024], BF16)
    rm0 = nc.dram_tensor("rm0", [D, TPC], F32)
    o0 = nc.dram_tensor("o0", [D, TPC], BF16)
    r1 = nc.dram_tensor("r1", [D, TPC], F32)
    q1 = nc.dram_tensor("q1", [D, TPC], BF16)
    k1 = nc.dram_tensor("k1", [D, TPC], BF16)
    v1 = nc.dram_tensor("v1", [TPC, D], BF16)
    agk1 = nc.dram_tensor("agk1", [8 * D, TPC], BF16)
    agv1 = nc.dram_tensor("agv1", [8 * TPC, D], BF16)
    rm1 = nc.dram_tensor("rm1", [D, TPC], F32)
    o1 = nc.dram_tensor("o1", [D, TPC], BF16)
    flin = nc.dram_tensor("flin", [1, 4], F32)
    flout = nc.dram_tensor("flout", [1, 4], F32)

    ctx = Ctx(nc, "F")
    em = ctx.em
    RG = [list(range(NCORES))]

    emit_phaseA(ctx, 0, dict(r_in=r0, wq_t=wq_t, wk_t=wk_t, wv_t=wv_t,
                             ropes=ropes, q_out=q0.ap(), k_out=k0.ap(),
                             v_out=v0.ap()))
    kvw = [(n, em.cnt[n]) for n in ("QSQ", "QSV") if em.cnt[n] > 0]
    ck = em.emit("gpsimd", lambda E: E.collective_compute(
        "AllGather", ALU.bypass, replica_groups=RG, ins=[k0.ap()],
        outs=[agk0.ap()]), waits=kvw, inc="CC0")
    cv = em.emit("gpsimd", lambda E: E.collective_compute(
        "AllGather", ALU.bypass, replica_groups=RG, ins=[v0.ap()],
        outs=[agv0.ap()]), inc="CC1")
    cf = em.emit("gpsimd", lambda E: E.collective_compute(
        "AllReduce", ALU.add, replica_groups=RG, ins=[flin.ap()],
        outs=[flout.ap()]), inc="CC1")
    ck, cv = cf, cf
    em.barrier()
    emit_phaseB(ctx, 0, dict(r_in=r0, q_in=q0.ap(), kg=agk0.ap(),
                             vg=agv0.ap(), wo_t=wo0_t, masks=masks,
                             wf1_t=wf1_t, r_mid=rm0.ap(), o_dram=o0.ap(),
                             r_out=r1.ap()),
                rank_major=True, kv_ready=[ck, cv])
    em.barrier()
    emit_phaseA(ctx, 1, dict(r_in=r1.ap(), wqkv_t=wqkv_t, lnp=lnp,
                             q_out=q1.ap(), k_out=k1.ap(), v_out=v1.ap()))
    kvw = [(n, em.cnt[n]) for n in ("KS0", "KS1", "QSV") if em.cnt[n] > 0]
    ck = em.emit("gpsimd", lambda E: E.collective_compute(
        "AllGather", ALU.bypass, replica_groups=RG, ins=[k1.ap()],
        outs=[agk1.ap()]), waits=kvw, inc="CC2")
    cv = em.emit("gpsimd", lambda E: E.collective_compute(
        "AllGather", ALU.bypass, replica_groups=RG, ins=[v1.ap()],
        outs=[agv1.ap()]), inc="CC3")
    cf = em.emit("gpsimd", lambda E: E.collective_compute(
        "AllReduce", ALU.add, replica_groups=RG, ins=[flin.ap()],
        outs=[flout.ap()]), inc="CC3")
    ck, cv = cf, cf
    em.barrier()
    emit_phaseB(ctx, 1, dict(r_in=r1.ap(), q_in=q1.ap(), kg=agk1.ap(),
                             vg=agv1.ap(), wo_t=wo1_t, masks=masks, albp=albp,
                             alr=alr, w12_t=w12_t, w3_t=w3_t, b12=b12, b3=b3,
                             r_mid=rm1.ap(), o_dram=o1.ap(), r_out=y_out),
                rank_major=True, kv_ready=[ck, cv])
    if debug:
        em.barrier()
        for t_, nm in [(q0, "dq0"), (k0, "dk0"), (v0, "dv0"), (agk0, "dagk0"),
                       (agv0, "dagv0"), (rm0, "drm0"), (o0, "do0"), (r1, "dr1"),
                       (q1, "dq1"), (k1, "dk1"), (v1, "dv1"), (rm1, "drm1"),
                       (o1, "do1")]:
            sh = list(t_.shape)
            dt_ = t_.dtype
            dst = nc.dram_tensor(nm, sh, dt_, kind="ExternalOutput").ap()
            em.dma("gpsimd", dst[:, :], t_.ap()[:, :], "QST")
    _final_barrier(em)
    em.finalize()
    return nc



FUSED = True

_programs = {}


def _get_prog(name, builder):
    if name not in _programs:
        _programs[name] = builder()
    return _programs[name]


def kernel(L0_attn_x, L0_attn_wq, L0_attn_wk, L0_attn_wv, L0_attn_wo,
           L0_ffn_identity, L1_attn_w_qkv, L1_attn_q_ln_w, L1_attn_q_ln_b,
           L1_attn_k_ln_w, L1_attn_k_ln_b, L1_attn_w_out, L1_ffn_w12,
           L1_ffn_b12, L1_ffn_w3, L1_ffn_b3):
    cores = list(range(NCORES))
    x = np.asarray(L0_attn_x, dtype=np.float32)[0]
    toks = [tok_ids(c) for c in cores]

    wq_t = tile_w(np.asarray(L0_attn_wq, np.float32))
    wk_t = tile_w(np.asarray(L0_attn_wk, np.float32))
    wv_t = tile_w(np.asarray(L0_attn_wv, np.float32))
    wo_t = tile_w(np.asarray(L0_attn_wo, np.float32))
    id_t = tile_w(np.ascontiguousarray(np.asarray(L0_ffn_identity,
                                                  np.float32).T))
    wqkv_t = tile_w(np.ascontiguousarray(np.asarray(L1_attn_w_qkv,
                                                    np.float32).T))
    wout_t = tile_w(np.ascontiguousarray(np.asarray(L1_attn_w_out,
                                                    np.float32).T))
    w12_t = tile_w(np.ascontiguousarray(np.asarray(L1_ffn_w12, np.float32).T))
    w3_t = tile_w(np.ascontiguousarray(np.asarray(L1_ffn_w3, np.float32).T))
    lnp = np.stack([np.asarray(L1_attn_q_ln_w, np.float32),
                    np.asarray(L1_attn_q_ln_b, np.float32),
                    np.asarray(L1_attn_k_ln_w, np.float32),
                    np.asarray(L1_attn_k_ln_b, np.float32)], axis=0)
    lnp = np.ascontiguousarray(
        lnp.reshape(4, 32, 128).transpose(2, 0, 1).reshape(128, 128))
    b12 = np.ascontiguousarray(np.asarray(L1_ffn_b12, np.float32)
                               .reshape(64, 128).T)
    b3 = np.ascontiguousarray(np.asarray(L1_ffn_b3, np.float32)
                              .reshape(32, 128).T)
    albp = alibi_lhs_packed()
    alrs = [alibi_rhs(c) for c in cores]
    msks = [causal_masks(c) for c in cores]
    ropes = []
    for c in cores:
        qc, qs = rope_tables(c, RS)
        kc_, ks = rope_tables(c, 1.0)
        ropes.append(np.stack([qc, qs, kc_, ks], axis=0))

    r_cur = [np.ascontiguousarray(x[toks[c]].T) for c in cores]

    def gather(shards, feat_major):
        if feat_major:
            KF = shards[0].shape[0]
            out = np.empty((KF, S), dtype=shards[0].dtype)
            for c in cores:
                out[:, toks[c]] = shards[c]
        else:
            KF = shards[0].shape[1]
            out = np.empty((S, KF), dtype=shards[0].dtype)
            for c in cores:
                out[toks[c], :] = shards[c]
        return out

    if FUSED:
        ncF = _get_prog("F", build_fused)
        in_maps = [dict(r_in=r_cur[c], wq_t=wq_t, wk_t=wk_t, wv_t=wv_t,
                        ropes=ropes[c], wo0_t=wo_t, masks=msks[c], wf1_t=id_t,
                        wqkv_t=wqkv_t, lnp=lnp, albp=albp, alr=alrs[c],
                        wo1_t=wout_t, w12_t=w12_t, w3_t=w3_t, b12=b12, b3=b3)
                   for c in cores]
        res = run_bass_kernel_spmd(ncF, in_maps, core_ids=cores)
        y = np.empty((1, S, D), dtype=np.float32)
        for c in cores:
            y[0, toks[c], :] = res.results[c]["y_out"].T
        return y

    for layer in (0, 1):
        ncA = _get_prog(f"A{layer}", lambda layer=layer: build_launchA(layer))
        if layer == 0:
            in_maps = [dict(r_in=r_cur[c], wq_t=wq_t, wk_t=wk_t, wv_t=wv_t,
                            ropes=ropes[c]) for c in cores]
        else:
            in_maps = [dict(r_in=r_cur[c], wqkv_t=wqkv_t, lnp=lnp)
                       for c in cores]
        resA = run_bass_kernel_spmd(ncA, in_maps, core_ids=cores)
        q_sh = [resA.results[c]["q_out"] for c in cores]
        kgf = gather([resA.results[c]["k_out"] for c in cores], True)
        vgf = gather([resA.results[c]["v_out"] for c in cores], False)

        ncB = _get_prog(f"B{layer}", lambda layer=layer: build_launchB(layer))
        if layer == 0:
            in_maps = [dict(r_in=r_cur[c], q_in=q_sh[c], kg=kgf, vg=vgf,
                            wo_t=wo_t, masks=msks[c], wf1_t=id_t)
                       for c in cores]
        else:
            in_maps = [dict(r_in=r_cur[c], q_in=q_sh[c], kg=kgf, vg=vgf,
                            wo_t=wout_t, masks=msks[c], albp=albp, alr=alrs[c],
                            w12_t=w12_t, w3_t=w3_t, b12=b12, b3=b3)
                       for c in cores]
        resB = run_bass_kernel_spmd(ncB, in_maps, core_ids=cores)
        r_cur = [resB.results[c]["r_out"] for c in cores]

    y = np.empty((1, S, D), dtype=np.float32)
    for c in cores:
        y[0, toks[c], :] = r_cur[c].T
    return y
